# version 03
# baseline (speedup 1.0000x reference)
import numpy as np
import concourse.bass as bass
import concourse.mybir as mybir
from concourse.bass_utils import run_bass_kernel_spmd

F32 = mybir.dt.float32
BF16 = mybir.dt.bfloat16
FP8 = mybir.dt.float8e4
U8 = mybir.dt.uint8
AF = mybir.ActivationFunctionType
ALU = mybir.AluOpType
AX = mybir.AxisListType

B, SEQ, D, NH, HD = 4, 4096, 1024, 16, 64
T = 2048
NT = 16
NC8 = 8
E, HF = 4, 2730
NHC = 22
HCS = [128] * 21 + [42]
GP = [585 * m for m in range(8)]
EPS = 1e-5
RECIP_D = 1.0 / 1024.0

_ENGS = ("sync", "tensor", "vector", "scalar", "gpsimd")


class Sem:
    def __init__(self, h):
        self.h = h
        self.v = 0

    def inc(self, n=1):
        self.v += n
        return self.v


class Prog:
    def __init__(self, nc):
        self.nc = nc
        self.ops = {k: [] for k in _ENGS}
        self._bar = None
        self._nbar = 0

    def sem(self, name):
        return Sem(self.nc.alloc_semaphore(name))

    def on(self, e, fn):
        self.ops[e].append(fn)

    def wait(self, e, sem, v):
        if v > 0:
            self.on(e, lambda eng, s=sem.h, v=v: eng.wait_ge(s, v))

    def barrier(self):
        if self._bar is None:
            self._bar = self.sem("bar")
        self._nbar += 1
        tgt = 5 * self._nbar
        for e in _ENGS:
            def f(eng, tgt=tgt):
                eng.sem_inc(self._bar.h, 1)
                eng.wait_ge(self._bar.h, tgt)
            self.on(e, f)

    def build_block(self):
        nc = self.nc
        with nc.Block() as block:
            @block.sync
            def _(eng):
                for f in self.ops["sync"]:
                    f(eng)

            @block.tensor
            def _(eng):
                for f in self.ops["tensor"]:
                    f(eng)

            @block.vector
            def _(eng):
                for f in self.ops["vector"]:
                    f(eng)

            @block.scalar
            def _(eng):
                for f in self.ops["scalar"]:
                    f(eng)

            @block.gpsimd
            def _(eng):
                for f in self.ops["gpsimd"]:
                    f(eng)



def _rope_tables(half):
    pos = np.arange(half * T, (half + 1) * T, dtype=np.float64)
    i = np.arange(32, dtype=np.float64)
    theta = 10000.0 ** (-2.0 * i / HD)
    ang = np.outer(theta, pos)
    c, s = np.cos(ang), np.sin(ang)
    Ct = np.empty((128, T), np.float64)
    St = np.empty((128, T), np.float64)
    for p in range(128):
        j = p % 64
        f = j % 32
        Ct[p] = c[f]
        St[p] = -s[f] if j < 32 else s[f]
    return Ct.astype(np.float32), St.astype(np.float32)


def _perm_qk():
    p = []
    for h in range(NH):
        p += [64 * h + 2 * i for i in range(32)]
        p += [64 * h + 2 * i + 1 for i in range(32)]
    return np.array(p, dtype=np.int64)


def _mswap():
    m = np.zeros((128, 128), np.float32)
    for k in range(128):
        m[k, k ^ 32] = 1.0
    return m


def _e16():
    m = np.zeros((D, NH), np.float32)
    for r in range(D):
        c, p = divmod(r, 128)
        m[r, 2 * c + p // 64] = 1.0
    return m.reshape(8, 128, 16)


def _gmask(half):
    m = np.ones((8, T), np.float32)
    for gi, gpos in enumerate(GP):
        for q in range(T):
            if gpos // 64 == (half * T + q) // 64:
                m[gi, q] = 0.0
    return m


def _gsel(half):
    m = np.zeros((64, T), np.uint8)
    for gpos in GP:
        if half * T <= gpos < (half + 1) * T:
            m[:, gpos - half * T] = 1
    return m


def _locmask():
    blk = np.zeros((128, 128), np.float32)
    blk[:64, :64] = 1.0
    blk[64:, 64:] = 1.0
    return np.tile(blk, (1, 4))



class CutEmission(Exception):
    pass


class K:

    CUT = None
    DBG = None

    def _cut(self, n):
        if self.CUT is not None and n >= self.CUT:
            raise CutEmission()

    def _dbg(self, name, src_ap, sem=None, v=None):
        if self.DBG != name:
            return
        P = self.P
        n = src_ap.size()
        assert n <= 2048 * 1024, n
        dst = self.stream[self.segs[0]][1].ap().flatten()[0:n]
        sh = src_ap.shape
        if len(sh) == 2:
            dst = dst.rearrange("(p x) -> p x", p=sh[0])
        else:
            dst = dst.rearrange("(p a x) -> p a x", p=sh[0], a=sh[1])
        if sem is not None:
            P.wait("gpsimd", sem, v if v is not None else sem.v)
        self.dma("gpsimd", dst, src_ap, self.so)
        P.wait("gpsimd", self.so, self.so.v)
        raise CutEmission()

    def __init__(self, segs=("l0a", "moe", "l1a", "l1f"), fused=True):
        nc = bass.Bass("TRN2", target_bir_lowering=False)
        self.nc = nc
        self.P = Prog(nc)
        self.segs = segs
        self.dram = {}
        self._declare_dram(fused)
        self._alloc_sbuf()
        self._alloc_psum()
        self._alloc_sems()

    def _declare_dram(self, fused):
        nc = self.nc
        dr = self.dram

        def di(name, shape, dtype=F32):
            dr[name] = nc.dram_tensor(name, shape, dtype, kind="ExternalInput")

        segs = self.segs
        names = []
        for s in segs:
            names.append(s)
        di("x0", [T, D])
        self.stream = {}
        prev = dr["x0"]
        for i, s in enumerate(segs):
            if i == len(segs) - 1:
                nxt = nc.dram_tensor("out", [T, D], F32, kind="ExternalOutput")
            elif fused:
                nxt = nc.dram_tensor("s_" + s, [T, D], F32, kind="Internal")
            else:
                nxt = nc.dram_tensor("out", [T, D], F32, kind="ExternalOutput")
            self.stream[s] = (prev, nxt)
            prev = nxt

        if "l0a" in segs or "l1a" in segs:
            di("ropeC", [128, T])
            di("ropeS", [128, T])
            di("mswap", [128, 128])
            di("e16", [8, 128, 16])
            di("gmask", [8, T])
            di("gsel", [64, T], U8)
            di("hsel", [128, T], U8)
            di("locmask", [128, 512])
            dr["kfm_own"] = nc.dram_tensor("kfm_own", [D, T], BF16, kind="Internal")
            dr["kfm_all"] = nc.dram_tensor("kfm_all", [2 * D, T], BF16, kind="Internal")
            dr["vhm_own"] = nc.dram_tensor("vhm_own", [NH * T, HD], BF16, kind="Internal")
            dr["vhm_all"] = nc.dram_tensor("vhm_all", [2 * NH * T, HD], BF16, kind="Internal")
            dr["kmx_own"] = nc.dram_tensor("kmx_own", [16, 1], F32, kind="Internal")
            dr["kmx_all"] = nc.dram_tensor("kmx_all", [32, 1], F32, kind="Internal")
        if "l0a" in segs:
            di("wq", [D, D])
            di("wk", [D, D])
            di("wv", [D, D])
            di("qb", [D])
            di("kb", [D])
            di("vb", [D])
        if "moe" in segs:
            di("gw", [E, D, HF])
            di("uw", [E, D, HF])
            di("dw", [E, HF, D])
            di("gb", [E, HF])
            di("ub", [E, HF])
            di("db", [E, D])
            di("rw", [D, E])
        if "l1a" in segs:
            di("l1wq", [D, D])
            di("l1wk", [D, D])
            di("l1wv", [D, D])
            di("l1wo", [D, D])
            di("l1bq", [D])
            di("l1bk", [D])
            di("l1bv", [D])
            di("l1bo", [D])
        if "l1f" in segs:
            di("w8T", [D, D])
            di("l1fb", [D])
            di("iscale", [1])
            di("wscale", [1])

    def _alloc_sbuf(self):
        nc = self.nc
        at = nc.alloc_sbuf_tensor_at
        base = (nc.sbuf_base + 31) & ~31
        o = [base]

        def place(name, shape, dtype, nbytes=None):
            if nbytes is None:
                per = int(np.prod(shape[1:])) * mybir.dt.size(dtype)
                nbytes = (per + 31) & ~31
            t = at(name, shape, dtype, offset=o[0])
            o[0] += nbytes
            return t

        self.ident = place("ident", [128, 128], BF16)
        self.mswap = place("mswapT", [128, 128], BF16)
        self.e16 = place("e16T", [128, 128], BF16)
        self.ones_r = place("ones_r", [1, 128], F32)
        self.bias8 = place("bias8", [128, 48], F32)
        self.vrow = place("vrowT", [1, 1024], F32)
        self.gub = place("gubT", [128, 176], F32)
        self.db4 = place("db4T", [4, 1024], BF16)
        self.kmx = place("kmxT", [16, 8], F32)
        self.sstat = place("sstat", [128, 24], F32)
        self.srow = place("srowT", [1, 16], F32)
        self.xin = place("xinT", [128, 2048], F32)
        self.xbf = place("xbfT", [128, 2048], BF16)
        self.atrash = at("atrash", [128, 1024], F32, offset=o[0] - 4096)
        self.wqk = place("wqkT", [128, 2048], BF16)
        self.wfull = place("wfullT", [128, 8192], BF16)
        self.scr = place("scrT", [128, 2048], F32)
        self.hsel = place("hselT", [128, 2048], U8)
        self.wts = place("wtsT", [128, 64], F32)
        self.wtsb = place("wtsbT", [128, 64], BF16)
        self.logi = place("logiT", [128, 16], F32)
        self.rwsb = place("rwsbT", [128, 32], BF16)
        self.wtst = place("wtstT", [4, 2048], BF16)

        fixed_end = o[0]

        pa0 = o[0]
        self.xfm = at("xfm", [128, 16384], BF16, offset=pa0)
        self.qp = at("qp", [128, 16384], BF16, offset=pa0 + 32768)
        oA = pa0 + 65536
        self.k65 = [at(f"k65_{i}", [65, 4096], BF16, offset=oA + 8192 * i) for i in range(2)]
        oA += 16384
        self.v65 = [at(f"v65_{i}", [128, 2560], BF16, offset=oA + 5120 * i) for i in range(2)]
        oA += 10240
        self.q65 = [at(f"q65_{i}", [65, 2048], BF16, offset=oA + 4096 * i) for i in range(2)]
        oA += 8192
        self.pt = [at(f"pt_{i}", [128, 1024], BF16, offset=oA + 2048 * i) for i in range(2)]
        oA += 4096
        self.ropec = at("ropec", [128, 2048], BF16, offset=oA)
        self.ropes = at("ropes", [128, 2048], BF16, offset=oA + 4096)
        self.bfb = at("bfbT", [128, 1024], F32, offset=oA)
        self.xq8 = at("xq8T", [128, 1024], FP8, offset=oA + 4096)
        self.msk = at("mskT", [128, 1024], U8, offset=oA + 5120)
        oA += 8192
        self.gall = at("gall", [16, 2048], BF16, offset=oA)
        oA += 4096
        self.q2sb = at("q2sb", [16, 2048], F32, offset=oA)
        self.k65loc = at("k65loc", [65, 2048], BF16, offset=oA)
        self.v65loc = at("v65loc", [128, 1280], BF16, offset=oA + 4096)
        oA += 8192
        self.k65g = at("k65g", [65, 16], BF16, offset=oA)
        self.v65g = at("v65g", [8, 80], BF16, offset=oA + 64)
        self.q65g = at("q65g", [65, 8], BF16, offset=oA + 64 + 160)
        self.recip = at("recip", [1, 2048], F32, offset=oA + 512)
        oA += 512 + 8192
        self.locmask = at("locmaskT", [128, 512], BF16, offset=oA)
        self.gmask = at("gmaskT", [8, 2048], BF16, offset=oA + 1024)
        self.gsel = at("gselT", [64, 2048], U8, offset=oA + 5120)
        self.ptm = at("ptmT", [128, 256], BF16, offset=oA + 7168)
        self.wtile = at("wtileT", [64, 2048], BF16, offset=oA + 7680)
        self.wrow = at("wrowT", [1, 2048], F32, offset=oA + 11776)
        oA += 19968
        self.atm = at("atm", [128, 16384], BF16, offset=pa0 + 32768)

        oB = pa0 + 32768
        self.hid = at("hid", [128, 22528], BF16, offset=oB)
        self.macc = at("macc", [128, 8192], F32, offset=oB + 45056)
        self.wdt = at("wdt", [128, 22528], BF16, offset=oB + 77824)
        oB = oB + 122880

        end = max(oA, oB)
        assert end <= nc.sbuf_top, f"sbuf over: fixed={fixed_end} end={end} top={nc.sbuf_top}"
        self.sbuf_end = end

    def _alloc_psum(self):
        nc = self.nc
        self.pa = nc.alloc_psum_tensor("pa", [128, 1024], F32)
        self.pb = nc.alloc_psum_tensor("pb", [128, 1024], F32)
        self.pc = nc.alloc_psum_tensor("pc", [128, 1024], F32)
        self.pd = nc.alloc_psum_tensor("pd", [128, 512], F32)
        self.pe = nc.alloc_psum_tensor("pe", [128, 1024], BF16)

    def _alloc_sems(self):
        self._dv_chain = (None, 0)
        P = self.P
        self.sx = P.sem("sx")
        self.st = P.sem("st")
        self.sw = P.sem("sw")
        self.skv = P.sem("skv")
        self.sg = P.sem("sg")
        self.scc = P.sem("scc")
        self.so = P.sem("so")
        self.sPE = P.sem("sPE")
        self.sDV = P.sem("sDV")
        self.sAC = P.sem("sAC")
        self.sPL = P.sem("sPL")

    def dma(self, q, out, in_, sem, slow=False):
        def f(eng, out=out, in_=in_, slow=slow, s=sem.h):
            eng.dma_start(out=out, in_=in_, allow_slow_non_contiguous=slow).then_inc(s, 16)
        self.P.on(q, f)
        return sem.inc(16)

    def dmat(self, out, in_, sem):
        def f(eng, out=out, in_=in_, s=sem.h):
            eng.dma_start_transpose(out, in_).then_inc(s, 16)
        self.P.on("sync", f)
        return sem.inc(16)

    def mm(self, out, lhsT, rhs, start=True, stop=True, sem=None, tp=None):
        s = sem.h if sem is not None else None

        def f(eng, out=out, l=lhsT, r=rhs, st=start, sp=stop, s=s, tp=tp):
            i = eng.matmul(out, l, r, start=st, stop=sp, tile_position=tp)
            if s is not None:
                i.then_inc(s, 1)
        self.P.on("tensor", f)
        return sem.inc() if sem is not None else None

    def ve(self, name, *args, sem=None, eng_name="vector", **kw):
        tiny = False
        if eng_name == "vector":
            for a in args:
                if hasattr(a, "free_size"):
                    try:
                        if a.free_size() <= 64 or a.partition_size() <= 16:
                            tiny = True
                            break
                    except Exception:
                        pass
        if tiny:
            cs, cv = self._dv_chain
            if cs is not None:
                self.P.wait("vector", cs, cv)
            if sem is None:
                sem = self.sDV
        s = sem.h if sem is not None else None

        def f(eng, name=name, args=args, kw=kw, s=s):
            i = getattr(eng, name)(*args, **kw)
            if s is not None:
                i.then_inc(s, 1)
        self.P.on(eng_name, f)
        v = sem.inc() if sem is not None else None
        if tiny:
            self._dv_chain = (sem, v)
        return v

    def act(self, out, in_, func, bias=0.0, scale=1.0, accum_out=None, sem=None):
        s = sem.h if sem is not None else None

        def f(eng, out=out, in_=in_, func=func, b=bias, sc=scale, ao=accum_out, s=s):
            i = eng.activation(out, in_, func, bias=b, scale=sc, accum_out=ao)
            if s is not None:
                i.then_inc(s, 1)
        self.P.on("scalar", f)
        return sem.inc() if sem is not None else None

    def emit_setup(self):
        P, dr = self.P, self.dram
        def mkident(eng):
            eng.memset(self.ident[:], 1.0)
        P.on("gpsimd", mkident)

        def mkident2(eng):
            eng.affine_select(
                out=self.ident[:], in_=self.ident[:],
                pattern=[[-1, 128]], compare_op=ALU.is_equal, fill=0.0,
                base=0, channel_multiplier=1,
            ).then_inc(self.sg.h, 1)
        P.on("gpsimd", mkident2)
        self.sg.inc(1)

        def mkones(eng):
            eng.memset(self.ones_r[:], 1.0)
            eng.memset(self.sstat[:, 10:11], 1e-30)
            eng.memset(self.sstat[:, 11:12], EPS)
            eng.memset(self.v65[0].ap().rearrange("p (c w) -> p c w", w=80)[:, :, 64:65], 1.0)
            eng.memset(self.v65[1].ap().rearrange("p (c w) -> p c w", w=80)[:, :, 64:65], 1.0)
            eng.memset(self.k65[0][64:65, :], 1.0)
            eng.memset(self.k65[1][64:65, :], 1.0)
            eng.memset(self.v65g[:, 64:65], 1.0)
            eng.sem_inc(self.sPL.h, 1)
        P.on("vector", mkones)
        self.sPL.inc(1)

        if "l0a" in self.segs or "l1a" in self.segs:
            self.dma("gpsimd", self.mswap[:], dr["mswap"].ap(), self.sx)
            self.dma("gpsimd", self.e16.ap().rearrange("p (c e) -> p c e", c=8),
                     dr["e16"].ap().rearrange("c p e -> p c e"), self.sx)
        self.setup_sx = self.sx.v

    def stage_fm(self, src, dst_view_fn, quant=False):
        P = self.P
        sx, st = self.sx, self.st
        xin3 = self.xin.ap().rearrange("p (r x) -> p r x", r=2)
        xbf3 = self.xbf.ap().rearrange("p (r x) -> p r x", r=2)
        src3 = src.ap().rearrange("(k p) d -> p k d", p=128)
        for tc in range(NT):
            r = tc % 2
            if tc >= 2:
                P.wait("sync", self.sDV, self._stage_cast[r])
            self.dma("sync", xin3[:, r], src3[:, tc], sx)
            P.wait("vector", sx, sx.v)
            if tc >= 2:
                P.wait("vector", st, self._stage_tp[r])
            if quant:
                self._emit_quant(xin3[:, r], xbf3[:, r])
                cast_v = self.sDV.v
            else:
                cast_v = self.ve("tensor_copy", xbf3[:, r], xin3[:, r], sem=self.sDV)
            if not hasattr(self, "_stage_cast"):
                pass
            self._stage_cast[r] = cast_v
            P.wait("sync", self.sDV, cast_v)
            self.dmat(dst_view_fn(tc), xbf3[:, r], st)
            self._stage_tp[r] = st.v

    def begin_stage(self):
        self._stage_cast = [0, 0]
        self._stage_tp = [0, 0]

    def _emit_quant(self, xin_ap, out_bf):
        scr = self.scr
        s0 = scr[:, 0:1024]
        s1 = scr[:, 1024:2048]
        ts = self.ve
        ts("tensor_scalar", s0, xin_ap, self.sstat[:, 8:9], None, ALU.mult)
        ts("tensor_scalar", s0, s0, 448.0, -448.0, ALU.min, ALU.max)
        ts("tensor_mul", xin_ap, s0, s0)
        ts("tensor_scalar", s1, s0, 240.0, -240.0, ALU.min, ALU.max)
        ts("tensor_copy", self.xq8[:], s1)
        ts("tensor_copy", out_bf, self.xq8[:])
        ts("tensor_scalar", s1, s0, 24576.0, -24576.0, ALU.add, ALU.add)
        ts("tensor_single_scalar", self.msk[:], xin_ap, 0.000244140625, ALU.is_lt)
        ts("select", out_bf, self.msk[:], s1, out_bf)
        ts("tensor_scalar", s1, s0, 402653184.0, -402653184.0, ALU.add, ALU.add)
        ts("tensor_single_scalar", self.msk[:], xin_ap, 61504.0, ALU.is_ge)
        ts("select", out_bf, self.msk[:], s1, out_bf, sem=self.sDV)

    def resid_ln_store(self, src3, dst3, tc, add_ap, r):
        P = self.P
        xin3 = self.xin.ap().rearrange("p (q x) -> p q x", q=2)
        s0 = self.scr[:, 1024 * r:1024 * (r + 1)]
        st = self.sstat[:, 8 * r - 8 if False else 0:]
        c = 16 * 0
        o = 0 if r == 0 else 4
        stS = self.sstat[:, o + 0:o + 1]
        stQ = self.sstat[:, o + 1:o + 2]
        stM = self.sstat[:, o + 2:o + 3]
        stR = self.sstat[:, o + 3:o + 4]
        stV = self.sstat[:, 12 + r:13 + r]
        stM2 = self.sstat[:, 14 + r:15 + r]
        P.wait("sync", self.so, self._ln_store[r])
        self.dma("sync", xin3[:, r], src3[:, tc], self.sx)
        P.wait("vector", self.sx, self.sx.v)
        addv = self.ve("tensor_add", s0, xin3[:, r], add_ap, sem=self.sDV)
        self.ve("reduce_sum", stS, s0, axis=AX.X)
        self.ve("tensor_scalar", stM, stS, RECIP_D, None, ALU.mult)
        self.ve("tensor_mul", stM2, stM, stM)
        self.ve("tensor_mul", self.atrash[:], s0, s0)
        self.ve("reduce_sum", stQ, self.atrash[:], axis=AX.X)
        v = self.ve("scalar_tensor_tensor", stV, stQ, RECIP_D,
                    stM2, ALU.mult, ALU.subtract, sem=self.sDV)
        P.wait("scalar", self.sDV, v)
        self.act(stR, stV, AF.Ln, bias=self.sstat[:, 11:12])
        self.act(stR, stR, AF.Exp, scale=-0.5)
        a = self.act(self.sstat[:, 16 + r:17 + r], stR, AF.Identity, sem=self.sAC)
        P.wait("vector", self.sAC, a)
        v = self.ve("tensor_scalar", xin3[:, r], s0, stM, stR,
                    ALU.subtract, ALU.mult, sem=self.sDV)
        P.wait("sync", self.sDV, v)
        self.dma("sync", dst3[:, tc], xin3[:, r], self.so)
        self._ln_store[r] = self.so.v
        return addv

    def emit_attn(self, layer):
        P, dr = self.P, self.dram
        l0 = layer == 0
        src, dst = self.stream["l0a" if l0 else "l1a"]
        wnq = "wq" if l0 else "l1wq"
        wnk = "wk" if l0 else "l1wk"
        wnv = "wv" if l0 else "l1wv"
        bq_col, bk_col = (0, 8) if l0 else (16, 24)
        sx, st, sw, skv, sg, scc = self.sx, self.st, self.sw, self.skv, self.sg, self.scc
        sPE, sDV, sAC, sPL = self.sPE, self.sDV, self.sAC, self.sPL
        xfm3 = self.xfm.ap().rearrange("p (c t) -> p c t", c=8)
        src3 = src.ap().rearrange("(k p) d -> p k d", p=128)
        dst3 = dst.ap().rearrange("(k p) d -> p k d", p=128)
        wqk3 = self.wqk.ap().rearrange("p (r x) -> p r x", r=2)
        xbf = self.xbf

        if l0:
            self.dma("gpsimd", self.ropec[:], dr["ropeC"].ap(), sx)
            self.dma("gpsimd", self.ropes[:], dr["ropeS"].ap(), sx)
        else:
            self.dma("gpsimd", self.locmask[:], dr["locmask"].ap(), sx)
            self.dma("gpsimd", self.gmask[:], dr["gmask"].ap(), sx)
            self.dma("sync", self.gsel[:], dr["gsel"].ap(), sx)
            self.dma("sync", self.hsel[:], dr["hsel"].ap(), sx)
        consts_v = sx.v

        def reinit(eng):
            eng.memset(self.v65[0].ap().rearrange("p (c w) -> p c w", w=80)[:, :, 64:65], 1.0)
            eng.memset(self.v65[1].ap().rearrange("p (c w) -> p c w", w=80)[:, :, 64:65], 1.0)
            eng.memset(self.k65[0][64:65, :], 1.0)
            eng.memset(self.k65[1][64:65, :], 1.0)
            eng.memset(self.v65g[:, 64:65], 1.0)
            if not l0:
                eng.memset(self.wtile[:], 0.0)
                eng.memset(self.wrow[:], 0.0)
            eng.sem_inc(self.sPL.h, 1)
        P.on("vector", reinit)
        ones_v = self.sPL.inc(1)

        self.dma("gpsimd", self.bias8[:, bq_col:bq_col + 8],
                 dr["qb" if l0 else "l1bq"].ap().rearrange("(c p) -> p c", p=128),
                 sw, slow=True)
        self.dma("gpsimd", self.bias8[:, bk_col:bk_col + 8],
                 dr["kb" if l0 else "l1bk"].ap().rearrange("(c p) -> p c", p=128),
                 sw, slow=True)
        if not l0:
            self.dma("gpsimd", self.bias8[:, 32:40],
                     dr["l1bo"].ap().rearrange("(c p) -> p c", p=128), sw, slow=True)
        self.dma("sync", self.vrow[:], dr["vb" if l0 else "l1bv"].ap().unsqueeze(0), sx)
        vrow_v = sx.v
        self.dma("gpsimd", self.wfull[:],
                 dr[wnv].ap().rearrange("(c p) o -> p c o", p=128), sw)
        wv_v = sw.v

        self.begin_stage()
        self._cut(1)
        self.stage_fm(src, lambda tc: xfm3[:, :, 128 * tc:128 * (tc + 1)])
        stage_done = self.st.v
        self._cut(2)
        self._dbg("xfm", self.xfm.ap(), self.st, stage_done)

        def proj_side(wname, bias_col):
            pa, pb, pc = self.pa, self.pb, self.pc
            qp3 = self.qp.ap().rearrange("p (c t) -> p c t", c=8)
            last_consumer = [0]
            e16s = [0, 0]
            chunk_done = []
            for oc in range(8):
                self._cut(2.1 + oc / 100.0)
                r = oc % 2
                wsrc = dr[wname].ap().rearrange("(c p) o -> p c o", p=128)[:, :, 128 * oc:128 * (oc + 1)]
                if oc >= 2:
                    P.wait("gpsimd", sPE, self._wqk_used[r])
                wv_ = self.dma("gpsimd", wqk3[:, r], wsrc, sw)
                P.wait("tensor", st, stage_done)
                P.wait("tensor", sw, wv_)
                if oc > 0:
                    P.wait("tensor", sDV if l0 else sAC, last_consumer[0])
                for cc in range(8):
                    lhs = wqk3[:, r][:, 128 * cc:128 * (cc + 1)]
                    for tb in range(4):
                        ps = (pa, pa, pb, pb)[tb]
                        col = (0, 512, 0, 512)[tb]
                        sem = sPE if (cc == 7 and tb == 3) else None
                        self.mm(ps[:, col:col + 512],
                                lhs, xfm3[:, cc][:, 512 * tb:512 * (tb + 1)],
                                start=(cc == 0), stop=(cc == 7), sem=sem)
                proj_v = sPE.v
                self._wqk_used[r] = proj_v
                P.wait("scalar", sPE, proj_v)
                P.wait("scalar", st, stage_done)
                bcol = self.bias8[:, bias_col + oc:bias_col + oc + 1]
                if l0:
                    self.act(xbf[:, 0:1024], pa[:], AF.Identity, bias=bcol)
                    ev = self.act(xbf[:, 1024:2048], pb[:], AF.Identity, bias=bcol, sem=sAC)
                    P.wait("tensor", sAC, ev)
                    P.wait("tensor", sPL, 1)
                    for tb in range(4):
                        ps = (pa, pa, pb, pb)[tb]
                        col = (0, 512, 0, 512)[tb]
                        self.mm(ps[:, col:col + 512], self.mswap[:],
                                xbf[:, 512 * tb:512 * (tb + 1)],
                                start=True, stop=True,
                                sem=sPE if tb == 3 else None)
                    sw_v = sPE.v
                    P.wait("vector", sAC, ev)
                    self.ve("tensor_mul", self.scr[:], xbf[:], self.ropec[:])
                    P.wait("vector", sPE, sw_v)
                    self.ve("tensor_mul", qp3[:, oc][:, 0:1024], pa[:], self.ropes[:, 0:1024])
                    v = self.ve("tensor_mul", qp3[:, oc][:, 1024:2048], pb[:],
                                self.ropes[:, 1024:2048], sem=sDV)
                    last_consumer[0] = v
                    v = self.ve("tensor_add", qp3[:, oc], qp3[:, oc], self.scr[:], sem=sDV)
                    cd_sem, cd = sDV, v
                else:
                    self.act(qp3[:, oc][:, 0:1024], pa[:], AF.Identity, bias=bcol)
                    ev = self.act(qp3[:, oc][:, 1024:2048], pb[:], AF.Identity, bias=bcol, sem=sAC)
                    last_consumer[0] = ev
                    cd_sem, cd = sAC, ev
                chunk_done.append((cd_sem, cd))
                self._cut(2.5 + oc / 100.0)
                P.wait("scalar", cd_sem, cd)
                if l0:
                    P.wait("scalar", sPE, sw_v)
                sq = self.act(xbf[:], qp3[:, oc], AF.Square, sem=sAC)
                P.wait("tensor", sAC, sq)
                e16l = self.e16[:, 16 * oc:16 * (oc + 1)]
                for half in range(2):
                    P.wait("tensor", sDV, max(e16s))
                    self.mm(pc[0:16, 0:512], e16l, xbf[:, 1024 * half:1024 * half + 512],
                            start=True, stop=True)
                    v = self.mm(pc[0:16, 512:1024], e16l,
                                xbf[:, 1024 * half + 512:1024 * (half + 1)],
                                start=True, stop=True, sem=sPE)
                    P.wait("vector", sPE, v)
                    if oc == 0 and half == 0:
                        pass
                    if oc == 0:
                        v = self.ve("tensor_copy", self.q2sb[0:16, 1024 * half:1024 * (half + 1)],
                                    pc[0:16, :], sem=sDV)
                    else:
                        v = self.ve("tensor_add", self.q2sb[0:16, 1024 * half:1024 * (half + 1)],
                                    self.q2sb[0:16, 1024 * half:1024 * (half + 1)],
                                    pc[0:16, :], sem=sDV)
                    e16s[half] = v
            return chunk_done, e16s[1], sq

        self._wqk_used = [0, 0]
        k_chunks, k_q2_done, k_sq_last = proj_side(wnk, bk_col)
        self._cut(3)
        self._dbg("kchunks", self.qp.ap(), self.sDV if l0 else self.sAC)
        self._dbg("k2", self.q2sb[0:16, :], self.sDV, k_q2_done)
        qp3 = self.qp.ap().rearrange("p (c t) -> p c t", c=8)
        for oc in range(8):
            cs, cv = k_chunks[oc]
            P.wait("sync", cs, cv)
            self.dma("sync", dr["kfm_own"].ap()[128 * oc:128 * (oc + 1), :], qp3[:, oc], skv)
        kfm_done = skv.v
        P.wait("vector", self.sDV, k_q2_done)
        v = self.ve("reduce_max", self.kmx[:, 0:1], self.q2sb[0:16, :], axis=AX.X, sem=sDV)
        P.wait("sync", sDV, v)
        self.dma("sync", dr["kmx_own"].ap(), self.kmx[0:16, 0:1], skv)
        kmx_dma = skv.v
        self._cut(3.5)

        def coll(eng, ins, outs, s=scc.h):
            eng.collective_compute(
                "AllGather", ALU.bypass,
                replica_groups=[[0, 1], [2, 3], [4, 5], [6, 7]],
                ins=[ins], outs=[outs],
            ).then_inc(s)

        P.wait("gpsimd", skv, kfm_done)
        P.on("gpsimd", lambda eng: coll(eng, dr["kfm_own"].ap().opt(), dr["kfm_all"].ap().opt()))
        scc.inc(1)
        cc_k = scc.v
        self._cut(3.7)
        P.wait("gpsimd", skv, kmx_dma)
        P.on("gpsimd", lambda eng: coll(eng, dr["kmx_own"].ap().opt(), dr["kmx_all"].ap().opt()))
        scc.inc(1)
        cc_km = scc.v
        self._cut(4)

        P.wait("tensor", sx, vrow_v)
        P.wait("tensor", sPL, 1)
        P.wait("tensor", sDV, k_q2_done)
        self.mm(self.pc[:, 0:512], self.ones_r[0:1, 0:128], self.vrow[0:1, 0:512])
        v = self.mm(self.pc[:, 512:1024], self.ones_r[0:1, 0:128],
                    self.vrow[0:1, 512:1024], sem=sPE)
        P.wait("vector", sPE, v)
        vb_v = self.ve("tensor_copy", self.scr[:, 0:1024], self.pc[:], sem=sDV)

        vst = [0, 0]
        evv = [0, 0]
        xbf2 = xbf.ap().rearrange("p (r x) -> p r x", r=2)
        vhm4 = dr["vhm_own"].ap().rearrange("(h k p) d -> p k h d", p=128, k=16)
        for tc in range(NT):
            self._cut(4.1 + tc / 100.0)
            r = tc % 2
            pp = self.pa if r == 0 else self.pb
            if tc < 2:
                P.wait("tensor", sw, wv_v)
                P.wait("tensor", sDV, vb_v)
            P.wait("tensor", sDV, evv[r])
            for cc in range(8):
                lhs = xfm3[:, cc][:, 128 * tc:128 * (tc + 1)]
                self.mm(pp[:, 0:512], lhs, self.wfull[:, 1024 * cc:1024 * cc + 512],
                        start=(cc == 0), stop=(cc == 7))
                self.mm(pp[:, 512:1024], lhs, self.wfull[:, 1024 * cc + 512:1024 * (cc + 1)],
                        start=(cc == 0), stop=(cc == 7),
                        sem=sPE if cc == 7 else None)
            pv_ = sPE.v
            P.wait("vector", sPE, pv_)
            P.wait("vector", skv, max(vst))
            if tc < 2:
                P.wait("vector", self.sAC, k_sq_last)
            ev = self.ve("tensor_add", xbf2[:, r], pp[:], self.scr[:, 0:1024], sem=sDV)
            evv[r] = ev
            P.wait("sync", sDV, ev)
            self.dma("sync", vhm4[:, tc],
                     xbf.ap().rearrange("p (r h d) -> p r h d", r=2, h=16)[:, r], skv)
            vst[r] = skv.v
        vhm_done = skv.v
        P.wait("gpsimd", skv, vhm_done)
        P.on("gpsimd", lambda eng: coll(eng, dr["vhm_own"].ap().opt(), dr["vhm_all"].ap().opt()))
        scc.inc(1)
        cc_v = scc.v
        self._cut(5)

        P.wait("scalar", skv, vhm_done)
        P.wait("vector", skv, vhm_done)
        self._dbg("bias8", self.bias8[:], self.sw)
        q_chunks, q_q2_done, _ = proj_side(wnq, bq_col)
        self._dbg("qpsum", self.qp.ap(), self.sDV if l0 else self.sAC)

        P.wait("sync", scc, cc_km)
        self.dma("sync", self.kmx[:, 1:2], dr["kmx_all"].ap()[0:16, :], skv)
        v = self.dma("sync", self.kmx[:, 2:3], dr["kmx_all"].ap()[16:32, :], skv)
        P.wait("vector", skv, v)
        self.ve("tensor_max", self.kmx[:, 3:4], self.kmx[:, 1:2], self.kmx[:, 2:3])
        self.ve("tensor_max", self.kmx[:, 3:4], self.kmx[:, 3:4], self.kmx[:, 0:1])
        P.wait("vector", sDV, q_q2_done)
        v = self.ve("tensor_scalar", self.q2sb[0:16, :], self.q2sb[0:16, :],
                    self.kmx[0:16, 3:4], None, ALU.mult, sem=sDV)
        P.wait("scalar", sDV, v)
        self.act(self.scr[0:16, :], self.q2sb[0:16, :], AF.Ln, bias=self.sstat[0:16, 10:11])
        a = self.act(self.q2sb[0:16, :], self.scr[0:16, :], AF.Exp, scale=0.5, sem=sAC)
        P.wait("vector", sAC, a)
        gv = self.ve("tensor_scalar", self.gall[0:16, :], self.q2sb[0:16, :],
                     -1.0, None, ALU.mult, sem=sDV)
        self._cut(6)
        self._dbg("qchunks", self.qp.ap(), self.sDV)
        self._dbg("gall", self.gall[0:16, :], self.sDV, gv)
        self._dbg("kfm", dr["kfm_all"].ap()[0:1024, :], self.scc, cc_k)
        self._dbg("vhm", dr["vhm_all"].ap()[0:16384, :], self.scc, cc_v)
        self._dbg("kmx", self.kmx[0:16, :], self.sDV)

        if l0:
            app_v = self._head_loop_l0(cc_k, cc_v, q_chunks, gv)
            atm_t, atm_src = self.atm, self.xfm
        else:
            app_v = self._head_loop_l1(cc_k, cc_v, q_chunks, gv)
            self._dbg("attnpre", self.xfm.ap(), self.sDV, app_v)
            app_v = self._oproj(app_v)
            atm_t, atm_src = self.xfm, self.qp
        self._cut(7)
        self._dbg("attnfm", self.xfm.ap() if l0 else self.qp.ap(), self.sDV if l0 else self.sAC)
        self._dbg("recip", self.recip[0:1, :], self.sDV)
        self._dbg("vhmo", dr["vhm_own"].ap()[30720:32768, :], self.scc)
        self._dbg("vhma", dr["vhm_all"].ap()[63488:65536, :], self.scc)
        self._dbg("v65", self.v65[1].ap(), self.sDV)
        self._dbg("k65", self.k65[1].ap(), self.sDV)
        self._dbg("q65", self.q65[1].ap(), self.sDV)
        af3s = atm_src.ap().rearrange("p (c t) -> p c t", c=8)
        atm3 = atm_t.ap().rearrange("p (k d) -> p k d", k=16)
        P.wait("sync", sDV if l0 else sAC, app_v)
        for c in range(8):
            self.dmat(atm3[:, :, 128 * c:128 * (c + 1)], af3s[:, c], st)
        tp_done = st.v
        P.wait("vector", st, tp_done)
        self._ln_store = [0, 0]
        for tc in range(NT):
            self.resid_ln_store(src3, dst3, tc,
                                atm_t[:, 1024 * tc:1024 * (tc + 1)], tc % 2)

    def _head_loop_l0(self, cc_k, cc_v, q_chunks, gv):
        P, dr = self.P, self.dram
        sPE, sDV, sAC, skv, sg, scc = self.sPE, self.sDV, self.sAC, self.skv, self.sg, self.scc
        af3 = self.xfm.ap().rearrange("p (c t) -> p c t", c=8)
        qp3 = self.qp.ap().rearrange("p (c t) -> p c t", c=8)
        kall, vall = self.dram["kfm_all"], self.dram["vhm_all"]
        vall5 = vall.ap().rearrange("(r h k p) d -> r h p k d", p=128, k=16, h=16)
        scores_end = {}
        pv_end = {}
        exp_par = [0, 0]
        evac_r_v = 0
        rapply_v = 0
        kv_done = {}
        qb_done = {}
        for h in range(16):
            self._cut(6.1 + h / 100.0)
            r = h % 2
            if h >= 2:
                P.wait("sync", sPE, scores_end[h - 2])
                P.wait("sync", sPE, pv_end[h - 2])
            else:
                P.wait("sync", scc, cc_k)
                P.wait("sync", scc, cc_v)
            self.dma("sync", self.k65[r][0:64, 0:2048], kall.ap()[64 * h:64 * h + 64, :], skv)
            self.dma("sync", self.k65[r][0:64, 2048:4096],
                     kall.ap()[1024 + 64 * h:1024 + 64 * h + 64, :], skv)
            v65v = self.v65[r].ap().rearrange("p (c w) -> p c w", w=80)
            self.dma("sync", v65v[:, 0:16, 0:64], vall5[0, h], skv)
            kvv = self.dma("sync", v65v[:, 16:32, 0:64], vall5[1, h], skv)
            kv_done[h] = kvv
            cs, cv = q_chunks[h // 2]
            P.wait("vector", cs, cv)
            if h >= 2:
                P.wait("vector", sPE, scores_end[h - 2])
            qv = self.ve("tensor_copy", self.q65[r][0:64, :],
                         qp3[:, h // 2][64 * (h % 2):64 * (h % 2) + 64, :], sem=sDV)
            P.wait("gpsimd", sDV, gv)
            if h >= 2:
                P.wait("gpsimd", sPE, scores_end[h - 2])
            grv = self.dma("gpsimd", self.q65[r][64:65, :], self.gall[h:h + 1, :], sg)
            qb_done[h] = (qv, grv)

            for p_ in range(2):
                self._cut(6.5 + h / 100.0 + p_ / 1000.0)
                sv = pvv = 0
                for kc in range(32):
                    par = kc % 2
                    sc = self.pa if par == 0 else self.pb
                    if kc < 2 and p_ == 0:
                        P.wait("tensor", skv, kv_done[h])
                        P.wait("tensor", sDV, qb_done[h][0])
                        P.wait("tensor", sg, qb_done[h][1])
                    P.wait("tensor", sAC, exp_par[par])
                    lhs = self.k65[r][0:65, 128 * kc:128 * (kc + 1)]
                    self.mm(sc[:, 0:512], lhs, self.q65[r][0:65, 1024 * p_:1024 * p_ + 512])
                    sv = self.mm(sc[:, 512:1024], lhs,
                                 self.q65[r][0:65, 1024 * p_ + 512:1024 * (p_ + 1)], sem=sPE)
                    P.wait("scalar", sPE, sv)
                    ev = self.act(self.pt[par][:], sc[:], AF.Exp, scale=0.125, sem=sAC)
                    exp_par[par] = ev
                    P.wait("tensor", sAC, ev)
                    if kc == 0:
                        P.wait("tensor", sDV, evac_r_v)
                    vl = self.v65[r][:, 80 * kc:80 * kc + 65]
                    self.mm(self.pc[0:65, 0:512], vl, self.pt[par][:, 0:512],
                            start=(kc == 0), stop=(kc == 31))
                    pvv = self.mm(self.pc[0:65, 512:1024], vl, self.pt[par][:, 512:1024],
                                  start=(kc == 0), stop=(kc == 31), sem=sPE)
                self._cut(6.7 + h / 100.0 + p_ / 1000.0)
                P.wait("vector", sPE, pvv)
                dsts = []
                for qb in range(2):
                    col = 512 * qb
                    dst = af3[:, h // 2][64 * (h % 2):64 * (h % 2) + 64,
                                         1024 * p_ + col:1024 * p_ + col + 512]
                    dsts.append(dst)
                    self.ve("tensor_copy", dst, self.pc[0:64, col:col + 512])
                    rv = self.ve("reciprocal",
                                 self.recip[0:1, 1024 * p_ + col:1024 * p_ + col + 512],
                                 self.pc[64:65, col:col + 512], sem=sDV)
                evac_r_v = rv
                for qb in range(2):
                    P.wait("tensor", sDV, rv if qb == 0 else rapply_v)
                    rmv = self.mm(self.pd[0:64, 0:512], self.ones_r[0:1, 0:64],
                                  self.recip[0:1, 1024 * p_ + 512 * qb:1024 * p_ + 512 * qb + 512],
                                  sem=sPE)
                    P.wait("vector", sPE, rmv)
                    rapply_v = self.ve("tensor_mul", dsts[qb], dsts[qb],
                                       self.pd[0:64, 0:512], sem=sDV)
            scores_end[h] = sv
            pv_end[h] = pvv
        return rapply_v

    def _head_loop_l1(self, cc_k, cc_v, q_chunks, gv):
        P, dr = self.P, self.dram
        sPE, sDV, sAC, sPL, skv, sg, scc = (self.sPE, self.sDV, self.sAC,
                                            self.sPL, self.skv, self.sg, self.scc)
        af3 = self.xfm.ap().rearrange("p (c t) -> p c t", c=8)
        qp3 = self.qp.ap().rearrange("p (c t) -> p c t", c=8)
        kall, vall = dr["kfm_all"], dr["vhm_all"]
        vall5 = vall.ap().rearrange("(r h k p) d -> r h p k d", p=128, k=16, h=16)
        vflat = vall.ap()
        head_end = {}
        exp_par = [0, 0]
        evac_v = {}
        rapply_v = 0
        mini_exp_v = 0
        kv_done = {}
        qb_done = {}
        merge_v = 0
        for h in range(16):
            self._cut(6.1 + h / 100.0)
            r = h % 2
            if h >= 2:
                P.wait("sync", sPE, head_end[h - 2])
            else:
                P.wait("sync", scc, cc_k)
                P.wait("sync", scc, cc_v)
            self.dma("sync", self.k65[r][0:64, 0:2048], kall.ap()[64 * h:64 * h + 64, :], skv)
            self.dma("sync", self.k65[r][0:64, 2048:4096],
                     kall.ap()[1024 + 64 * h:1024 + 64 * h + 64, :], skv)
            v65v = self.v65[r].ap().rearrange("p (c w) -> p c w", w=80)
            self.dma("sync", v65v[:, 0:16, 0:64], vall5[0, h], skv)
            kvv = self.dma("sync", v65v[:, 16:32, 0:64], vall5[1, h], skv)
            if h >= 1:
                P.wait("sync", sPE, head_end[h - 1])
            g4a = bass.AP(dr["vhm_all"], (2048 * h) * 64, [[585 * 64, 4], [1, 64]])
            g4b = bass.AP(dr["vhm_all"], (32768 + 2048 * h + 292) * 64, [[585 * 64, 4], [1, 64]])
            self.dma("sync", self.v65g[0:4, 0:64], g4a, skv)
            kvv = self.dma("sync", self.v65g[4:8, 0:64], g4b, skv)
            kv_done[h] = kvv
            cs, cv = q_chunks[h // 2]
            P.wait("vector", cs, cv)
            if h >= 2:
                P.wait("vector", sPE, head_end[h - 2])
            qv = self.ve("tensor_copy", self.q65[r][0:64, :],
                         qp3[:, h // 2][64 * (h % 2):64 * (h % 2) + 64, :], sem=sDV)
            P.wait("gpsimd", sDV, gv)
            if h >= 2:
                P.wait("gpsimd", sPE, head_end[h - 2])
            grv = self.dma("gpsimd", self.q65[r][64:65, :], self.gall[h:h + 1, :], sg)
            P.wait("vector", skv, kv_done[h])
            P.wait("vector", sg, grv)
            if h >= 1:
                P.wait("vector", sPE, head_end[h - 1])
            self.ve("select", self.k65loc[0:65, :], self.hsel[0:65, :],
                    self.k65[r][0:65, 2048:4096], self.k65[r][0:65, 0:2048])
            self.ve("select", self.v65loc[:, :], self.hsel[:, 0:1280],
                    self.v65[r][:, 1280:2560], self.v65[r][:, 0:1280])
            self.ve("tensor_copy", self.k65g[0:65, 0:8], self.k65[r][0:65, 0:4096:585])
            self.ve("tensor_copy", self.q65g[0:65, 0:4], self.q65[r][0:65, 0:2048:585])
            qgv = self.ve("tensor_copy", self.q65g[0:65, 4:8],
                          self.q65[r][0:65, 292:2048:585], sem=sDV)
            qb_done[h] = (qv, grv, qgv)
            P.wait("tensor", sDV, qgv)

            for qtg in range(4):
                self._cut(6.2 + h / 100.0 + qtg / 1000.0)
                par = qtg % 2
                col = 512 * par
                if qtg < 2:
                    P.wait("tensor", skv, kv_done[h])
                    P.wait("tensor", sDV, qb_done[h][2])
                    P.wait("tensor", sg, grv)
                    P.wait("tensor", sDV, merge_v)
                P.wait("tensor", sAC, exp_par[par])
                for j in range(4):
                    qt = 4 * qtg + j
                    self.mm(self.pa[:, col + 128 * j:col + 128 * (j + 1)],
                            self.k65loc[0:65, 128 * qt:128 * (qt + 1)],
                            self.q65[r][0:65, 128 * qt:128 * (qt + 1)])
                sv = self.mm(self.pb[0:8, col:col + 512], self.k65g[0:65, 0:8],
                             self.q65[r][0:65, 512 * qtg:512 * (qtg + 1)], sem=sPE)
                P.wait("scalar", sPE, sv)
                self.act(self.pt[par][:, 0:512], self.pa[:, col:col + 512],
                         AF.Exp, scale=0.125)
                ev = self.act(self.pt[par][0:8, 512:1024], self.pb[0:8, col:col + 512],
                              AF.Exp, scale=0.125, sem=sAC)
                exp_par[par] = ev
                P.wait("gpsimd", sAC, ev)
                self.ve("tensor_mul", self.pt[par][:, 0:512], self.pt[par][:, 0:512],
                        self.locmask[:], eng_name="gpsimd")
                mv = self.ve("tensor_mul", self.pt[par][0:8, 512:1024],
                             self.pt[par][0:8, 512:1024],
                             self.gmask[0:8, 512 * qtg:512 * (qtg + 1)],
                             sem=sPL, eng_name="gpsimd")
                P.wait("tensor", sPL, mv)
                if qtg >= 2:
                    P.wait("tensor", sDV, evac_v[qtg - 2])
                for j in range(4):
                    qt = 4 * qtg + j
                    self.mm(self.pc[0:65, col + 128 * j:col + 128 * (j + 1)],
                            self.v65loc[:, 80 * qt:80 * qt + 65],
                            self.pt[par][:, 128 * j:128 * (j + 1)],
                            start=(j == 0), stop=False)
                pvv = self.mm(self.pc[0:65, col:col + 512], self.v65g[0:8, 0:65],
                              self.pt[par][0:8, 512:1024],
                              start=False, stop=True, sem=sPE)
                P.wait("vector", sPE, pvv)
                dst = af3[:, h // 2][64 * (h % 2):64 * (h % 2) + 64,
                                     512 * qtg:512 * (qtg + 1)]
                self.ve("tensor_copy", dst, self.pc[0:64, col:col + 512])
                cvv = self.ve("tensor_copy",
                              self.recip[0:1, 512 * qtg:512 * (qtg + 1)],
                              self.pc[64:65, col:col + 512], sem=sDV)
                evac_v[qtg] = cvv

            self._cut(6.5 + h / 100.0)
            P.wait("tensor", sAC, mini_exp_v)
            P.wait("tensor", sDV, rapply_v)
            P.wait("tensor", sDV, qb_done[h][2])
            for kc in range(32):
                if kc % 8 == 0 and kc > 0:
                    P.wait("tensor", sAC, self.sAC.v)
                self.mm(self.pd[:, 8 * kc:8 * (kc + 1)], self.k65[r][0:65, 128 * kc:128 * (kc + 1)],
                        self.q65g[0:65, 0:8],
                        sem=sPE if kc % 8 == 7 else None)
                if kc % 8 == 7:
                    g = kc // 8
                    P.wait("scalar", sPE, self.sPE.v)
                    ae = self.act(self.ptm[:, 64 * g:64 * (g + 1)],
                                  self.pd[:, 64 * g:64 * (g + 1)],
                                  AF.Exp, scale=0.125, sem=sAC)
            mini_exp_v = ae
            P.wait("tensor", sDV, evac_v[2])
            P.wait("tensor", sDV, evac_v[3])
            for kc in range(32):
                if kc % 8 == 0:
                    P.wait("tensor", sAC, mini_exp_v - 3 + kc // 8)
                ogv = self.mm(self.pc[0:65, 0:8], self.v65[r][:, 80 * kc:80 * kc + 65],
                              self.ptm[:, 8 * kc:8 * (kc + 1)],
                              start=(kc == 0), stop=(kc == 31),
                              sem=sPE if kc == 31 else None)
            self._cut(6.6 + h / 100.0)
            P.wait("vector", sPE, ogv)
            self.ve("tensor_copy", self.wtile[0:64, 0:2048:585], self.pc[0:64, 0:4])
            self.ve("tensor_copy", self.wtile[0:64, 292:2048:585], self.pc[0:64, 4:8])
            self.ve("tensor_copy", self.wrow[0:1, 0:2048:585], self.pc[64:65, 0:4])
            merge_v = self.ve("tensor_copy", self.wrow[0:1, 292:2048:585],
                              self.pc[64:65, 4:8], sem=sDV)
            dsta = af3[:, h // 2][64 * (h % 2):64 * (h % 2) + 64, :]
            self.ve("select", dsta, self.gsel[0:64, :], self.wtile[0:64, :], dsta)
            self.ve("select", self.recip[0:1, :], self.gsel[0:1, :],
                    self.wrow[0:1, :], self.recip[0:1, :])
            rv = self.ve("reciprocal", self.scr[0:1, 0:2048], self.recip[0:1, :], sem=sDV)
            self._cut(6.7 + h / 100.0)
            for qb in range(4):
                P.wait("tensor", sDV, rv if qb == 0 else rapply_v)
                P.wait("tensor", sAC, mini_exp_v)
                rmv = self.mm(self.pd[0:64, 0:512], self.ones_r[0:1, 0:64],
                              self.scr[0:1, 512 * qb:512 * (qb + 1)], sem=sPE)
                P.wait("vector", sPE, rmv)
                dst = af3[:, h // 2][64 * (h % 2):64 * (h % 2) + 64,
                                     512 * qb:512 * (qb + 1)]
                rapply_v = self.ve("tensor_mul", dst, dst, self.pd[0:64, 0:512], sem=sDV)
            head_end[h] = rmv
        return rapply_v

    def _oproj(self, app_v):
        P, dr = self.P, self.dram
        sPE, sDV, sAC, sw = self.sPE, self.sDV, self.sAC, self.sw
        af3 = self.xfm.ap().rearrange("p (c t) -> p c t", c=8)
        op3 = self.qp.ap().rearrange("p (c t) -> p c t", c=8)
        wqk3 = self.wqk.ap().rearrange("p (r x) -> p r x", r=2)
        ev_prev = 0
        for oc in range(8):
            r = oc % 2
            wsrc = dr["l1wo"].ap().rearrange("(c p) o -> p c o", p=128)[:, :, 128 * oc:128 * (oc + 1)]
            if oc >= 2:
                P.wait("gpsimd", sPE, self._wqk_used[r])
            wv_ = self.dma("gpsimd", wqk3[:, r], wsrc, sw)
            P.wait("tensor", sw, wv_)
            if oc == 0:
                P.wait("tensor", sDV, app_v)
            else:
                P.wait("tensor", sAC, ev_prev)
            for cc in range(8):
                lhs = wqk3[:, r][:, 128 * cc:128 * (cc + 1)]
                for tb in range(4):
                    ps = (self.pa, self.pa, self.pb, self.pb)[tb]
                    col = (0, 512, 0, 512)[tb]
                    self.mm(ps[:, col:col + 512], lhs,
                            af3[:, cc][:, 512 * tb:512 * (tb + 1)],
                            start=(cc == 0), stop=(cc == 7),
                            sem=sPE if (cc == 7 and tb == 3) else None)
            pv_ = sPE.v
            self._wqk_used[r] = pv_
            P.wait("scalar", sPE, pv_)
            bcol = self.bias8[:, 32 + oc:33 + oc]
            self.act(op3[:, oc][:, 0:1024], self.pa[:], AF.Identity, bias=bcol)
            ev_prev = self.act(op3[:, oc][:, 1024:2048], self.pb[:], AF.Identity,
                               bias=bcol, sem=sAC)
        return ev_prev

    def emit_moe(self):
        P, dr = self.P, self.dram
        src, dst = self.stream["moe"]
        sx, st, sw, sg = self.sx, self.st, self.sw, self.sg
        sPE, sDV, sAC = self.sPE, self.sDV, self.sAC
        src3 = src.ap().rearrange("(k p) d -> p k d", p=128)
        dst3 = dst.ap().rearrange("(k p) d -> p k d", p=128)
        xm4 = self.xfm.ap().rearrange("p (s c t) -> p s c t", s=2, c=8)
        wqk3 = self.wqk.ap().rearrange("p (r x) -> p r x", r=2)
        xbf2 = self.xbf.ap().rearrange("p (r x) -> p r x", r=2)

        self.dma("gpsimd", self.rwsb.ap().rearrange("p (c e) -> p c e", c=8),
                 dr["rw"].ap().rearrange("(c p) e -> p c e", p=128), sw, slow=True)
        for e in range(E):
            self.dma("gpsimd", self.gub[:, 22 * e:22 * e + 21],
                     dr["gb"].ap()[e:e + 1, 0:2688].rearrange("a (c p) -> p a c", p=128).squeeze(1),
                     sw, slow=True)
            self.dma("gpsimd", self.gub[0:42, 22 * e + 21:22 * e + 22],
                     dr["gb"].ap()[e:e + 1, 2688:2730].rearrange("a p -> p a"),
                     sw, slow=True)
            self.dma("gpsimd", self.gub[:, 88 + 22 * e:88 + 22 * e + 21],
                     dr["ub"].ap()[e:e + 1, 0:2688].rearrange("a (c p) -> p a c", p=128).squeeze(1),
                     sw, slow=True)
            self.dma("gpsimd", self.gub[0:42, 88 + 22 * e + 21:88 + 22 * e + 22],
                     dr["ub"].ap()[e:e + 1, 2688:2730].rearrange("a p -> p a"),
                     sw, slow=True)
        cst_v = self.dma("gpsimd", self.db4[:], dr["db"].ap(), sw)

        self.begin_stage()
        self.stage_fm(src, lambda tc: xm4[:, tc // 8, :, 128 * (tc % 8):128 * (tc % 8 + 1)])
        stage_done = self.st.v

        top2_v = [0, 0]
        for tc in range(NT):
            r = tc % 2
            pp = self.pa if r == 0 else self.pb
            P.wait("tensor", st, stage_done)
            P.wait("tensor", sw, cst_v)
            P.wait("tensor", sDV, top2_v[r])
            for cc in range(8):
                self.mm(pp[:, 0:4], xm4[:, tc // 8, cc][:, 128 * (tc % 8):128 * (tc % 8 + 1)],
                        self.rwsb[:, 4 * cc:4 * (cc + 1)],
                        start=(cc == 0), stop=(cc == 7),
                        sem=sPE if cc == 7 else None)
            rv = self.sPE.v
            lg = self.logi
            L = pp[:, 0:4]
            P.wait("vector", sPE, rv)
            self.ve("reduce_max", lg[:, 0:1], L, axis=AX.X)
            self.ve("tensor_scalar", lg[:, 4:8], L, lg[:, 0:1], None, ALU.is_equal)
            self.ve("scalar_tensor_tensor", lg[:, 8:12], lg[:, 4:8], -1.0e30, L,
                    ALU.mult, ALU.add)
            self.ve("reduce_max", lg[:, 1:2], lg[:, 8:12], axis=AX.X)
            v = self.ve("tensor_scalar", lg[:, 2:3], lg[:, 0:1], -1.0, None,
                        ALU.mult, sem=sDV)
            P.wait("scalar", sDV, v)
            self.act(lg[:, 4:8], L, AF.Exp, bias=lg[:, 2:3])
            self.act(lg[:, 3:4], lg[:, 1:2], AF.Exp, bias=lg[:, 2:3])
            a = self.act(self.atrash[:, 0:5], lg[:, 3:8], AF.Identity, sem=sAC)
            P.wait("vector", sAC, a)
            self.ve("tensor_scalar", lg[:, 3:4], lg[:, 3:4], 1.0, None, ALU.add)
            self.ve("reciprocal", lg[:, 12:13], lg[:, 3:4])
            self.ve("tensor_scalar", lg[:, 8:12], L, lg[:, 1:2], None, ALU.is_ge)
            self.ve("tensor_scalar", self.wts[:, 4 * tc:4 * (tc + 1)], lg[:, 4:8],
                    lg[:, 12:13], None, ALU.mult)
            top2_v[r] = self.ve("tensor_mul", self.wts[:, 4 * tc:4 * (tc + 1)],
                                self.wts[:, 4 * tc:4 * (tc + 1)], lg[:, 8:12], sem=sDV)

        P.wait("vector", sDV, top2_v[0])
        P.wait("vector", sDV, top2_v[1])
        wb_v = self.ve("tensor_copy", self.wtsb[:], self.wts[:], sem=sDV)
        P.wait("tensor", sDV, wb_v)
        P.wait("tensor", self.sg, 1)
        for half in range(2):
            for j in range(8):
                tc = 8 * half + j
                def tr(eng, o=self.pe[0:4, 128 * j:128 * (j + 1)],
                       i=self.wtsb[:, 4 * tc:4 * (tc + 1)], idt=self.ident[:],
                       s=self.sPE.h, last=(j == 7)):
                    ins = eng.transpose(o, i, idt)
                    if last:
                        ins.then_inc(s, 1)
                P.on("tensor", tr)
            tv = self.sPE.inc()
            P.wait("vector", sPE, tv)
            ev = self.ve("tensor_copy", self.wtst[0:4, 1024 * half:1024 * (half + 1)],
                         self.pe[0:4, :], sem=sDV)
            if half == 0:
                P.wait("tensor", sDV, ev)
        wtst_v = ev

        hid3 = self.hid.ap().rearrange("p (c t) -> p c t", c=NHC)
        wd_end = 0
        down_ev = [0, 0]
        resid_v = 0
        gu_used = [0, 0]
        for th in range(2):
            for e in range(E):
                P.wait("gpsimd", sPE, wd_end)
                self.dma("gpsimd", self.wdt.ap()[:, 0:21504].rearrange("p (c o) -> p c o", c=21),
                         dr["dw"].ap()[e, 0:2688, :].rearrange("(c p) o -> p c o", p=128), sw)
                wd_v = self.dma("gpsimd", self.wdt[0:42, 21504:22528],
                                dr["dw"].ap()[e, 2688:2730, :], sw)
                silu_v = {}
                for hc in range(NHC):
                    hsz = HCS[hc]
                    r = hc % 2
                    pp = self.pa if r == 0 else self.pb
                    P.wait("gpsimd", sPE, gu_used[r])
                    wv_ = self.dma("gpsimd",
                                   self.wqk.ap()[:, 1024 * r:1024 * r + 8 * hsz].rearrange("p (c h) -> p c h", c=8),
                                   dr["gw"].ap()[e, :, 128 * hc:128 * hc + hsz].rearrange("(c p) h -> p c h", p=128),
                                   sw)
                    P.wait("tensor", sw, wv_)
                    P.wait("tensor", st, stage_done)
                    P.wait("tensor", sAC, silu_v.get(hc - 2, 0))
                    for cc in range(8):
                        lhs = self.wqk.ap()[:, 1024 * r + hsz * cc:1024 * r + hsz * (cc + 1)]
                        self.mm(pp[0:hsz, 0:512], lhs, xm4[:, th, cc][:, 0:512],
                                start=(cc == 0), stop=(cc == 7))
                        self.mm(pp[0:hsz, 512:1024], lhs, xm4[:, th, cc][:, 512:1024],
                                start=(cc == 0), stop=(cc == 7),
                                sem=sPE if cc == 7 else None)
                    gu_used[r] = self.sPE.v
                    P.wait("scalar", sPE, self.sPE.v)
                    silu_v[hc] = self.act(hid3[:, hc][0:hsz, :], pp[0:hsz, :], AF.Silu,
                                          bias=self.gub[0:hsz, 22 * e + hc:22 * e + hc + 1],
                                          sem=sAC)
                up_v = {}
                for hc in range(NHC):
                    hsz = HCS[hc]
                    r = hc % 2
                    pp = self.pa if r == 0 else self.pb
                    P.wait("gpsimd", sPE, gu_used[r])
                    wv_ = self.dma("gpsimd",
                                   self.wqk.ap()[:, 1024 * r:1024 * r + 8 * hsz].rearrange("p (c h) -> p c h", c=8),
                                   dr["uw"].ap()[e, :, 128 * hc:128 * hc + hsz].rearrange("(c p) h -> p c h", p=128),
                                   sw)
                    P.wait("tensor", sw, wv_)
                    prev = up_v.get(hc - 2)
                    if prev is not None:
                        P.wait("tensor", sDV, prev[0])
                        P.wait("tensor", sAC, prev[1])
                    for cc in range(8):
                        lhs = self.wqk.ap()[:, 1024 * r + hsz * cc:1024 * r + hsz * (cc + 1)]
                        self.mm(pp[0:hsz, 0:512], lhs, xm4[:, th, cc][:, 0:512],
                                start=(cc == 0), stop=(cc == 7))
                        self.mm(pp[0:hsz, 512:1024], lhs, xm4[:, th, cc][:, 512:1024],
                                start=(cc == 0), stop=(cc == 7),
                                sem=sPE if cc == 7 else None)
                    gu_used[r] = self.sPE.v
                    P.wait("scalar", sPE, self.sPE.v)
                    P.wait("scalar", sAC, silu_v[hc])
                    av = self.act(xbf2[:, r][0:hsz, :], pp[0:hsz, :], AF.Identity,
                                  bias=self.gub[0:hsz, 88 + 22 * e + hc:88 + 22 * e + hc + 1],
                                  sem=sAC)
                    P.wait("vector", sAC, av)
                    dv = self.ve("tensor_mul", hid3[:, hc][0:hsz, :], hid3[:, hc][0:hsz, :],
                                 xbf2[:, r][0:hsz, :], sem=sDV)
                    up_v[hc] = (dv, av)
                hid_done = up_v[NHC - 1][0]
                P.wait("tensor", sw, wd_v)
                P.wait("tensor", sDV, hid_done)
                for tc8 in range(8):
                    r = tc8 % 2
                    pp = self.pa if r == 0 else self.pb
                    P.wait("tensor", sDV, down_ev[r])
                    for hc in range(NHC):
                        hsz = HCS[hc]
                        lhs = hid3[:, hc][0:hsz, 128 * tc8:128 * (tc8 + 1)]
                        rhs = (self.wdt.ap()[:, 1024 * hc:1024 * (hc + 1)] if hc < 21
                               else self.wdt.ap()[0:42, 21504:22528])
                        self.mm(pp[:, 0:512], lhs, rhs[0:hsz, 0:512],
                                start=(hc == 0), stop=(hc == 21))
                        self.mm(pp[:, 512:1024], lhs, rhs[0:hsz, 512:1024],
                                start=(hc == 0), stop=(hc == 21),
                                sem=sPE if hc == 21 else None)
                    dv = self.sPE.v
                    P.wait("vector", sPE, dv)
                    if e > 0 or th > 0:
                        P.wait("vector", sDV, resid_v)
                    wcol = self.wts[:, 4 * (8 * th + tc8) + e:4 * (8 * th + tc8) + e + 1]
                    if e == 0:
                        down_ev[r] = self.ve(
                            "tensor_scalar", self.macc[:, 1024 * tc8:1024 * (tc8 + 1)],
                            pp[:], wcol, None, ALU.mult, sem=sDV)
                    else:
                        down_ev[r] = self.ve(
                            "scalar_tensor_tensor", self.macc[:, 1024 * tc8:1024 * (tc8 + 1)],
                            pp[:], wcol, self.macc[:, 1024 * tc8:1024 * (tc8 + 1)],
                            ALU.mult, ALU.add, sem=sDV)
                wd_end = self.sPE.v
            self._ln_store = getattr(self, "_ln_store", [0, 0])
            for tc8 in range(8):
                r = tc8 % 2
                pp = self.pa if r == 0 else self.pb
                P.wait("tensor", sDV, down_ev[r])
                P.wait("tensor", sDV, wtst_v)
                P.wait("tensor", sw, cst_v)
                self.mm(pp[:, 0:512], self.wtst[0:4, 128 * (8 * th + tc8):128 * (8 * th + tc8 + 1)],
                        self.db4[0:4, 0:512], start=True, stop=True)
                bv = self.mm(pp[:, 512:1024], self.wtst[0:4, 128 * (8 * th + tc8):128 * (8 * th + tc8 + 1)],
                             self.db4[0:4, 512:1024], start=True, stop=True, sem=sPE)
                P.wait("vector", sPE, bv)
                down_ev[r] = self.ve("tensor_add", self.macc[:, 1024 * tc8:1024 * (tc8 + 1)],
                                     self.macc[:, 1024 * tc8:1024 * (tc8 + 1)], pp[:], sem=sDV)
            for tc8 in range(8):
                tc = 8 * th + tc8
                resid_v = self.resid_ln_store(src3, dst3, tc,
                                              self.macc[:, 1024 * tc8:1024 * (tc8 + 1)], tc % 2)

    def emit_l1f(self):
        P, dr = self.P, self.dram
        src, dst = self.stream["l1f"]
        sx, st, sw = self.sx, self.st, self.sw
        sPE, sDV, sAC = self.sPE, self.sDV, self.sAC
        src3 = src.ap().rearrange("(k p) d -> p k d", p=128)
        dst3 = dst.ap().rearrange("(k p) d -> p k d", p=128)
        xfm3 = self.xfm.ap().rearrange("p (c t) -> p c t", c=8)

        self.dma("sync", self.srow[0:1, 0:1], dr["iscale"].ap().unsqueeze(0), sx)
        sv = self.dma("sync", self.srow[0:1, 1:2], dr["wscale"].ap().unsqueeze(0), sx)
        P.wait("vector", sx, sv)
        self.ve("reciprocal", self.srow[0:1, 2:3], self.srow[0:1, 0:1])
        rv = self.ve("tensor_mul", self.srow[0:1, 3:4], self.srow[0:1, 0:1],
                     self.srow[0:1, 1:2], sem=sDV)
        P.wait("tensor", sDV, rv)
        P.wait("tensor", self.sPL, 1)
        self.mm(self.pd[:, 0:1], self.ones_r[0:1, 0:128], self.srow[0:1, 2:3])
        bv = self.mm(self.pd[:, 1:2], self.ones_r[0:1, 0:128], self.srow[0:1, 3:4], sem=sPE)
        P.wait("vector", sPE, bv)
        self.ve("tensor_copy", self.sstat[:, 8:9], self.pd[:, 0:1])
        scl_v = self.ve("tensor_copy", self.sstat[:, 9:10], self.pd[:, 1:2], sem=sDV)
        fb = self.dma("sync", self.vrow[:], dr["l1fb"].ap().unsqueeze(0), sx)
        P.wait("tensor", sx, fb)
        P.wait("tensor", sDV, scl_v)
        self.mm(self.pa[:, 0:512], self.ones_r[0:1, 0:128], self.vrow[0:1, 0:512])
        bv = self.mm(self.pa[:, 512:1024], self.ones_r[0:1, 0:128],
                     self.vrow[0:1, 512:1024], sem=sPE)
        P.wait("vector", sPE, bv)
        bfb_v = self.ve("tensor_copy", self.bfb[:], self.pa[:], sem=sDV)
        w8_v = self.dma("gpsimd", self.wfull[:],
                        dr["w8T"].ap().rearrange("(c p) o -> p c o", p=128), sw)

        P.wait("vector", sDV, scl_v)
        self.begin_stage()
        self.stage_fm(src, lambda tc: xfm3[:, :, 128 * tc:128 * (tc + 1)], quant=True)
        stage_done = self.st.v

        self._ln_store = getattr(self, "_ln_store", [0, 0])
        ev_prev = [0, 0]
        for tc in range(NT):
            r = tc % 2
            pp = self.pa if r == 0 else self.pb
            P.wait("tensor", st, stage_done)
            P.wait("tensor", sw, w8_v)
            P.wait("tensor", sDV, max(bfb_v, ev_prev[r]))
            for cc in range(8):
                lhs = xfm3[:, cc][:, 128 * tc:128 * (tc + 1)]
                self.mm(pp[:, 0:512], lhs, self.wfull[:, 1024 * cc:1024 * cc + 512],
                        start=(cc == 0), stop=(cc == 7))
                self.mm(pp[:, 512:1024], lhs, self.wfull[:, 1024 * cc + 512:1024 * (cc + 1)],
                        start=(cc == 0), stop=(cc == 7),
                        sem=sPE if cc == 7 else None)
            mv = self.sPE.v
            P.wait("vector", sPE, mv)
            ev = self.ve("scalar_tensor_tensor", self.scr[:, 0:1024], pp[:],
                         self.sstat[:, 9:10], self.bfb[:], ALU.mult, ALU.add, sem=sDV)
            ev_prev[r] = ev
            self.resid_ln_store(src3, dst3, tc, self.scr[:, 0:1024], r)

    def build(self):
        try:
            self.emit_setup()
            first = True
            for s in self.segs:
                if not first:
                    self.P.barrier()
                first = False
                if s == "l0a":
                    self.emit_attn(0)
                elif s == "moe":
                    self.emit_moe()
                elif s == "l1a":
                    self.emit_attn(1)
                elif s == "l1f":
                    self.emit_l1f()
        except CutEmission:
            pass
        self.P.wait("sync", self.so, self.so.v)
        self.P.barrier()
        self.P.build_block()
        return self.nc



_SEG_INPUTS = {
    "l0a": ["ropeC", "ropeS", "mswap", "e16", "gmask", "gsel", "hsel", "locmask",
            "wq", "wk", "wv", "qb", "kb", "vb"],
    "moe": ["gw", "uw", "dw", "gb", "ub", "db", "rw"],
    "l1a": ["ropeC", "ropeS", "mswap", "e16", "gmask", "gsel", "hsel", "locmask",
            "l1wq", "l1wk", "l1wv", "l1wo", "l1bq", "l1bk", "l1bv", "l1bo"],
    "l1f": ["w8T", "l1fb", "iscale", "wscale"],
}


def host_stage(inputs):
    PQK = _perm_qk()
    g = {}
    g["wq"] = np.ascontiguousarray(np.asarray(inputs["L0_attn_wq_w"])[PQK, :].T)
    g["wk"] = np.ascontiguousarray(np.asarray(inputs["L0_attn_wk_w"])[PQK, :].T)
    g["wv"] = np.ascontiguousarray(np.asarray(inputs["L0_attn_wv_w"]).T)
    g["qb"] = np.ascontiguousarray(np.asarray(inputs["L0_attn_wq_b"])[PQK])
    g["kb"] = np.ascontiguousarray(np.asarray(inputs["L0_attn_wk_b"])[PQK])
    g["vb"] = np.ascontiguousarray(np.asarray(inputs["L0_attn_wv_b"]))
    g["gw"] = np.ascontiguousarray(np.asarray(inputs["L0_ffn_gate_w"]))
    g["uw"] = np.ascontiguousarray(np.asarray(inputs["L0_ffn_up_w"]))
    g["dw"] = np.ascontiguousarray(np.asarray(inputs["L0_ffn_down_w"]))
    g["gb"] = np.ascontiguousarray(np.asarray(inputs["L0_ffn_gate_b"]))
    g["ub"] = np.ascontiguousarray(np.asarray(inputs["L0_ffn_up_b"]))
    g["db"] = np.ascontiguousarray(np.asarray(inputs["L0_ffn_down_b"]))
    g["rw"] = np.ascontiguousarray(np.asarray(inputs["L0_ffn_router_w"]))
    g["l1wq"] = np.ascontiguousarray(np.asarray(inputs["L1_attn_Wq"]))
    g["l1wk"] = np.ascontiguousarray(np.asarray(inputs["L1_attn_Wk"]))
    g["l1wv"] = np.ascontiguousarray(np.asarray(inputs["L1_attn_Wv"]))
    g["l1wo"] = np.ascontiguousarray(np.asarray(inputs["L1_attn_Wo"]))
    g["l1bq"] = np.ascontiguousarray(np.asarray(inputs["L1_attn_bq"]))
    g["l1bk"] = np.ascontiguousarray(np.asarray(inputs["L1_attn_bk"]))
    g["l1bv"] = np.ascontiguousarray(np.asarray(inputs["L1_attn_bv"]))
    g["l1bo"] = np.ascontiguousarray(np.asarray(inputs["L1_attn_bo"]))
    g["w8T"] = np.ascontiguousarray(np.asarray(inputs["L1_ffn_weight_fp8"]).T)
    g["l1fb"] = np.ascontiguousarray(np.asarray(inputs["L1_ffn_bias"]))
    g["iscale"] = np.ascontiguousarray(np.asarray(inputs["L1_ffn_input_scale"]))
    g["wscale"] = np.ascontiguousarray(np.asarray(inputs["L1_ffn_weight_scale"]))
    g["mswap"] = _mswap()
    g["e16"] = np.ascontiguousarray(_e16())
    g["locmask"] = _locmask()
    per = []
    x = np.asarray(inputs["L0_attn_x"]).reshape(B, 2, T, D)
    for c in range(8):
        b, half = c // 2, c % 2
        d = {"x0": np.ascontiguousarray(x[b, half])}
        ct, stt = _rope_tables(half)
        d["ropeC"], d["ropeS"] = ct, stt
        d["gmask"] = _gmask(half)
        d["gsel"] = _gsel(half)
        d["hsel"] = np.full((128, T), half, np.uint8)
        per.append(d)
    return g, per


def _in_maps(k, g, per, segs):
    names = set(["x0"])
    for s in segs:
        names.update(_SEG_INPUTS[s])
    maps = []
    for c in range(8):
        m = {}
        for n in names:
            m[n] = per[c][n] if n in per[c] else g[n]
        maps.append(m)
    return maps


_ALL = ("l0a", "moe", "l1a", "l1f")
MODE = "fused"


def kernel(**inputs):
    g, per = host_stage(inputs)
    if MODE == "fused":
        k = K(_ALL, fused=True)
        nc = k.build()
        maps = _in_maps(k, g, per, _ALL)
        res = run_bass_kernel_spmd(nc, maps, core_ids=list(range(8)))
        outs = [res.results[c]["out"] for c in range(8)]
    else:
        cur = [per[c]["x0"] for c in range(8)]
        for seg in _ALL:
            k = K((seg,), fused=False)
            nc = k.build()
            maps = _in_maps(k, g, per, (seg,))
            for c in range(8):
                maps[c]["x0"] = cur[c]
            res = run_bass_kernel_spmd(nc, maps, core_ids=list(range(8)))
            cur = [res.results[c]["out"] for c in range(8)]
        outs = cur
    out = np.empty((B, SEQ, D), np.float32)
    for c in range(8):
        out[c // 2, (c % 2) * T:(c % 2 + 1) * T, :] = outs[c]
    return out


# version 06
# speedup vs baseline: 1.3603x; 1.3603x over previous
import numpy as np
import concourse.bass as bass
import concourse.mybir as mybir
from concourse.bass_utils import run_bass_kernel_spmd

F32 = mybir.dt.float32
BF16 = mybir.dt.bfloat16
FP8 = mybir.dt.float8e4
U8 = mybir.dt.uint8
AF = mybir.ActivationFunctionType
ALU = mybir.AluOpType
AX = mybir.AxisListType

B, SEQ, D, NH, HD = 4, 4096, 1024, 16, 64
T = 2048
NT = 16
NC8 = 8
E, HF = 4, 2730
NHC = 22
HCS = [128] * 21 + [42]
GP = [585 * m for m in range(8)]
EPS = 1e-5
RECIP_D = 1.0 / 1024.0

_ENGS = ("sync", "tensor", "vector", "scalar", "gpsimd")


class Sem:
    def __init__(self, h):
        self.h = h
        self.v = 0

    def inc(self, n=1):
        self.v += n
        return self.v


class Prog:
    def __init__(self, nc):
        self.nc = nc
        self.ops = {k: [] for k in _ENGS}
        self._bar = None
        self._nbar = 0

    def sem(self, name):
        return Sem(self.nc.alloc_semaphore(name))

    def on(self, e, fn):
        self.ops[e].append(fn)

    def wait(self, e, sem, v):
        if v > 0:
            self.on(e, lambda eng, s=sem.h, v=v: eng.wait_ge(s, v))

    def barrier(self):
        if self._bar is None:
            self._bar = self.sem("bar")
        self._nbar += 1
        tgt = 5 * self._nbar
        for e in _ENGS:
            def f(eng, tgt=tgt):
                eng.sem_inc(self._bar.h, 1)
                eng.wait_ge(self._bar.h, tgt)
            self.on(e, f)

    def build_block(self):
        nc = self.nc
        with nc.Block() as block:
            @block.sync
            def _(eng):
                for f in self.ops["sync"]:
                    f(eng)

            @block.tensor
            def _(eng):
                for f in self.ops["tensor"]:
                    f(eng)

            @block.vector
            def _(eng):
                for f in self.ops["vector"]:
                    f(eng)

            @block.scalar
            def _(eng):
                for f in self.ops["scalar"]:
                    f(eng)

            @block.gpsimd
            def _(eng):
                for f in self.ops["gpsimd"]:
                    f(eng)



def _rope_tables(half):
    pos = np.arange(half * T, (half + 1) * T, dtype=np.float64)
    i = np.arange(32, dtype=np.float64)
    theta = 10000.0 ** (-2.0 * i / HD)
    ang = np.outer(theta, pos)
    c, s = np.cos(ang), np.sin(ang)
    Ct = np.empty((128, T), np.float64)
    St = np.empty((128, T), np.float64)
    for p in range(128):
        j = p % 64
        f = j % 32
        Ct[p] = c[f]
        St[p] = -s[f] if j < 32 else s[f]
    return Ct.astype(np.float32), St.astype(np.float32)


def _perm_qk():
    p = []
    for h in range(NH):
        p += [64 * h + 2 * i for i in range(32)]
        p += [64 * h + 2 * i + 1 for i in range(32)]
    return np.array(p, dtype=np.int64)


def _mswap():
    m = np.zeros((128, 128), np.float32)
    for k in range(128):
        m[k, k ^ 32] = 1.0
    return m


def _e16():
    m = np.zeros((D, NH), np.float32)
    for r in range(D):
        c, p = divmod(r, 128)
        m[r, 2 * c + p // 64] = 1.0
    return m.reshape(8, 128, 16)


def _gmask(half):
    m = np.ones((8, T), np.float32)
    for gi, gpos in enumerate(GP):
        for q in range(T):
            if gpos // 64 == (half * T + q) // 64:
                m[gi, q] = 0.0
    return m


def _gsel(half):
    m = np.zeros((64, T), np.uint8)
    for gpos in GP:
        if half * T <= gpos < (half + 1) * T:
            m[:, gpos - half * T] = 1
    return m


def _locmask():
    blk = np.zeros((128, 128), np.float32)
    blk[:64, :64] = 1.0
    blk[64:, 64:] = 1.0
    return np.tile(blk, (1, 4))



class CutEmission(Exception):
    pass


class K:

    CUT = None
    DBG = None

    def _cut(self, n):
        if self.CUT is not None and n >= self.CUT:
            raise CutEmission()

    def _dbg(self, name, src_ap, sem=None, v=None):
        if self.DBG != name:
            return
        P = self.P
        n = src_ap.size()
        assert n <= 2048 * 1024, n
        dst = self.stream[self.segs[0]][1].ap().flatten()[0:n]
        sh = src_ap.shape
        if len(sh) == 2:
            dst = dst.rearrange("(p x) -> p x", p=sh[0])
        else:
            dst = dst.rearrange("(p a x) -> p a x", p=sh[0], a=sh[1])
        if sem is not None:
            P.wait("gpsimd", sem, v if v is not None else sem.v)
        self.dma("gpsimd", dst, src_ap, self.so)
        P.wait("gpsimd", self.so, self.so.v)
        raise CutEmission()

    def __init__(self, segs=("l0a", "moe", "l1a", "l1f"), fused=True):
        nc = bass.Bass("TRN2", target_bir_lowering=False)
        self.nc = nc
        self.P = Prog(nc)
        self.segs = segs
        self.dram = {}
        self._declare_dram(fused)
        self._alloc_sbuf()
        self._alloc_psum()
        self._alloc_sems()

    def _declare_dram(self, fused):
        nc = self.nc
        dr = self.dram

        def di(name, shape, dtype=F32):
            dr[name] = nc.dram_tensor(name, shape, dtype, kind="ExternalInput")

        segs = self.segs
        names = []
        for s in segs:
            names.append(s)
        di("x0", [T, D])
        self.stream = {}
        prev = dr["x0"]
        for i, s in enumerate(segs):
            if i == len(segs) - 1:
                nxt = nc.dram_tensor("out", [T, D], F32, kind="ExternalOutput")
            elif fused:
                nxt = nc.dram_tensor("s_" + s, [T, D], F32, kind="Internal")
            else:
                nxt = nc.dram_tensor("out", [T, D], F32, kind="ExternalOutput")
            self.stream[s] = (prev, nxt)
            prev = nxt

        if "l0a" in segs or "l1a" in segs:
            di("ropeC", [128, T])
            di("ropeS", [128, T])
            di("mswap", [128, 128])
            di("e16", [8, 128, 16])
            di("gmask", [8, T])
            di("gsel", [64, T], U8)
            di("hsel", [128, T], U8)
            di("locmask", [128, 512])
            dr["kfm_own"] = nc.dram_tensor("kfm_own", [D, T], BF16, kind="Internal")
            dr["kfm_all"] = nc.dram_tensor("kfm_all", [2 * D, T], BF16, kind="Internal")
            dr["vhm_own"] = nc.dram_tensor("vhm_own", [NH * T, HD], BF16, kind="Internal")
            dr["vhm_all"] = nc.dram_tensor("vhm_all", [2 * NH * T, HD], BF16, kind="Internal")
            dr["kmx_own"] = nc.dram_tensor("kmx_own", [16, 1], F32, kind="Internal")
            dr["kmx_all"] = nc.dram_tensor("kmx_all", [32, 1], F32, kind="Internal")
        if "l0a" in segs:
            di("wq", [D, D])
            di("wk", [D, D])
            di("wv", [D, D])
            di("qb", [D])
            di("kb", [D])
            di("vb", [D])
        if "moe" in segs:
            di("gw", [E, D, HF])
            di("uw", [E, D, HF])
            di("dw", [E, HF, D])
            di("gb", [E, HF])
            di("ub", [E, HF])
            di("db", [E, D])
            di("rw", [D, E])
        if "l1a" in segs:
            di("l1wq", [D, D])
            di("l1wk", [D, D])
            di("l1wv", [D, D])
            di("l1wo", [D, D])
            di("l1bq", [D])
            di("l1bk", [D])
            di("l1bv", [D])
            di("l1bo", [D])
        if "l1f" in segs:
            di("w8T", [D, D])
            di("l1fb", [D])
            di("iscale", [1])
            di("wscale", [1])

    def _alloc_sbuf(self):
        nc = self.nc
        at = nc.alloc_sbuf_tensor_at
        base = (nc.sbuf_base + 31) & ~31
        o = [base]

        def place(name, shape, dtype, nbytes=None):
            if nbytes is None:
                per = int(np.prod(shape[1:])) * mybir.dt.size(dtype)
                nbytes = (per + 31) & ~31
            t = at(name, shape, dtype, offset=o[0])
            o[0] += nbytes
            return t

        self.ident = place("ident", [128, 128], BF16)
        self.mswap = place("mswapT", [128, 128], BF16)
        self.e16 = place("e16T", [128, 128], BF16)
        self.ones_r = place("ones_r", [1, 128], F32)
        self.bias8 = place("bias8", [128, 48], F32)
        self.vrow = place("vrowT", [1, 1024], F32)
        self.gub = place("gubT", [128, 176], F32)
        self.kmx = place("kmxT", [16, 8], F32)
        self.sstat = place("sstat", [128, 24], F32)
        self.srow = place("srowT", [1, 16], F32)
        self.xin = place("xinT", [128, 2048], F32)
        self.xbf = place("xbfT", [128, 2048], BF16)
        self.atrash = at("atrash", [128, 1024], F32, offset=o[0] - 4096)
        self.wqk = place("wqkT", [128, 4096], BF16)
        self.wfull = place("wfullT", [128, 8192], BF16)
        self.db4 = at("db4T", [4, 1024], BF16, offset=o[0] - 16384)
        self.scr = place("scrT", [128, 2048], F32)
        self.wts = place("wtsT", [128, 64], F32)
        self.wtsb = place("wtsbT", [128, 64], BF16)
        self.logi = place("logiT", [128, 16], F32)
        self.rwsb = place("rwsbT", [128, 32], BF16)
        self.wtst = place("wtstT", [4, 2048], BF16)

        fixed_end = o[0]

        pa0 = o[0]
        self.xfm = at("xfm", [128, 16384], BF16, offset=pa0)
        self.qp = at("qp", [128, 16384], BF16, offset=pa0 + 32768)
        oA = pa0 + 65536
        self.k65 = [at(f"k65_{i}", [65, 4096], BF16, offset=oA + 8192 * i) for i in range(2)]
        oA += 16384
        self.v65 = [at(f"v65_{i}", [128, 2560], BF16, offset=oA + 5120 * i) for i in range(2)]
        oA += 10240
        self.q65 = [at(f"q65_{i}", [65, 2048], BF16, offset=oA + 4096 * i) for i in range(2)]
        oA += 8192
        self.pt = [at(f"pt_{i}", [128, 1024], BF16, offset=oA + 2048 * i) for i in range(2)]
        oA += 4096
        self.ropec = at("ropec", [128, 2048], BF16, offset=oA)
        self.ropes = at("ropes", [128, 2048], BF16, offset=oA + 4096)
        self.bfb = at("bfbT", [128, 1024], F32, offset=oA)
        self.xq8 = at("xq8T", [128, 1024], FP8, offset=oA + 4096)
        self.msk = at("mskT", [128, 1024], U8, offset=oA + 5120)
        self.hsel = at("hselT", [128, 2048], U8, offset=oA + 6144)
        oA += 8192
        self.gall = at("gall", [16, 2048], BF16, offset=oA)
        oA += 4096
        self.q2sb = at("q2sb", [16, 2048], F32, offset=oA)
        self.k65loc = at("k65loc", [65, 2048], BF16, offset=oA)
        self.v65loc = at("v65loc", [128, 1280], BF16, offset=oA + 4096)
        oA += 8192
        self.k65g = at("k65g", [65, 16], BF16, offset=oA)
        self.v65g = at("v65g", [8, 80], BF16, offset=oA + 64)
        self.q65g = at("q65g", [65, 8], BF16, offset=oA + 64 + 160)
        self.recip = at("recip", [1, 2048], F32, offset=oA + 512)
        oA += 512 + 8192
        self.locmask = at("locmaskT", [128, 512], BF16, offset=oA)
        self.gmask = at("gmaskT", [8, 2048], BF16, offset=oA + 1024)
        self.gsel = at("gselT", [64, 2048], U8, offset=oA + 5120)
        self.ptm = at("ptmT", [128, 256], BF16, offset=oA + 7168)
        self.wtile = at("wtileT", [64, 2048], BF16, offset=oA + 7680)
        self.wrow = at("wrowT", [1, 2048], F32, offset=oA + 11776)
        oA += 19968
        self.atm = at("atm", [128, 16384], BF16, offset=pa0 + 32768)

        oB = pa0 + 32768
        self.hid = at("hid", [128, 22528], BF16, offset=oB)
        self.macc = at("macc", [128, 8192], F32, offset=oB + 45056)
        self.wdt = at("wdt", [128, 22528], BF16, offset=oB + 77824)
        oB = oB + 122880

        end = max(oA, oB)
        assert end <= nc.sbuf_top, f"sbuf over: fixed={fixed_end} end={end} top={nc.sbuf_top}"
        self.sbuf_end = end

    def _alloc_psum(self):
        nc = self.nc
        self.pa = nc.alloc_psum_tensor("pa", [128, 1024], F32)
        self.pb = nc.alloc_psum_tensor("pb", [128, 1024], F32)
        self.pc = nc.alloc_psum_tensor("pc", [128, 1024], F32)
        self.pd = nc.alloc_psum_tensor("pd", [128, 512], F32)
        self.pe = nc.alloc_psum_tensor("pe", [128, 1024], BF16)

    def _alloc_sems(self):
        self._dv_chain = (None, 0)
        P = self.P
        self.sx = P.sem("sx")
        self.st = P.sem("st")
        self.sw = P.sem("sw")
        self.skv = P.sem("skv")
        self.sg = P.sem("sg")
        self.scc = P.sem("scc")
        self.so = P.sem("so")
        self.sPE = P.sem("sPE")
        self.sDV = P.sem("sDV")
        self.sAC = P.sem("sAC")
        self.sPL = P.sem("sPL")

    def dma(self, q, out, in_, sem, slow=False):
        def f(eng, out=out, in_=in_, slow=slow, s=sem.h):
            eng.dma_start(out=out, in_=in_, allow_slow_non_contiguous=slow).then_inc(s, 16)
        self.P.on(q, f)
        return sem.inc(16)

    def dmat(self, out, in_, sem):
        def f(eng, out=out, in_=in_, s=sem.h):
            eng.dma_start_transpose(out, in_).then_inc(s, 16)
        self.P.on("sync", f)
        return sem.inc(16)

    def mm(self, out, lhsT, rhs, start=True, stop=True, sem=None, tp=None):
        s = sem.h if sem is not None else None

        def f(eng, out=out, l=lhsT, r=rhs, st=start, sp=stop, s=s, tp=tp):
            i = eng.matmul(out, l, r, start=st, stop=sp, tile_position=tp)
            if s is not None:
                i.then_inc(s, 1)
        self.P.on("tensor", f)
        return sem.inc() if sem is not None else None

    def ve(self, name, *args, sem=None, eng_name="vector", **kw):
        tiny = False
        if eng_name == "vector":
            for a in args:
                if hasattr(a, "free_size"):
                    try:
                        if a.free_size() <= 64 or a.partition_size() <= 16:
                            tiny = True
                            break
                    except Exception:
                        pass
        if tiny:
            cs, cv = self._dv_chain
            if cs is not None:
                self.P.wait("vector", cs, cv)
            if sem is None:
                sem = self.sDV
        s = sem.h if sem is not None else None

        def f(eng, name=name, args=args, kw=kw, s=s):
            i = getattr(eng, name)(*args, **kw)
            if s is not None:
                i.then_inc(s, 1)
        self.P.on(eng_name, f)
        v = sem.inc() if sem is not None else None
        if tiny:
            self._dv_chain = (sem, v)
        return v

    def act(self, out, in_, func, bias=0.0, scale=1.0, accum_out=None, sem=None):
        s = sem.h if sem is not None else None

        def f(eng, out=out, in_=in_, func=func, b=bias, sc=scale, ao=accum_out, s=s):
            i = eng.activation(out, in_, func, bias=b, scale=sc, accum_out=ao)
            if s is not None:
                i.then_inc(s, 1)
        self.P.on("scalar", f)
        return sem.inc() if sem is not None else None

    def emit_setup(self):
        P, dr = self.P, self.dram
        def mkident(eng):
            eng.memset(self.ident[:], 1.0)
        P.on("gpsimd", mkident)

        def mkident2(eng):
            eng.affine_select(
                out=self.ident[:], in_=self.ident[:],
                pattern=[[-1, 128]], compare_op=ALU.is_equal, fill=0.0,
                base=0, channel_multiplier=1,
            ).then_inc(self.sg.h, 1)
        P.on("gpsimd", mkident2)
        self.sg.inc(1)

        def mkones(eng):
            eng.memset(self.ones_r[:], 1.0)
            eng.memset(self.sstat[:, 10:11], 1e-30)
            eng.memset(self.sstat[:, 11:12], EPS)
            eng.memset(self.v65[0].ap().rearrange("p (c w) -> p c w", w=80)[:, :, 64:65], 1.0)
            eng.memset(self.v65[1].ap().rearrange("p (c w) -> p c w", w=80)[:, :, 64:65], 1.0)
            eng.memset(self.k65[0][64:65, :], 1.0)
            eng.memset(self.k65[1][64:65, :], 1.0)
            eng.memset(self.v65g[:, 64:65], 1.0)
            eng.sem_inc(self.sPL.h, 1)
        P.on("vector", mkones)
        self.sPL.inc(1)

        if "l0a" in self.segs or "l1a" in self.segs:
            self.dma("gpsimd", self.mswap[:], dr["mswap"].ap(), self.sx)
            self.dma("gpsimd", self.e16.ap().rearrange("p (c e) -> p c e", c=8),
                     dr["e16"].ap().rearrange("c p e -> p c e"), self.sx)
        self.setup_sx = self.sx.v

    def stage_fm(self, src, dst_view_fn, quant=False):
        P = self.P
        sx, st = self.sx, self.st
        xin3 = self.xin.ap().rearrange("p (r x) -> p r x", r=2)
        xbf3 = self.xbf.ap().rearrange("p (r x) -> p r x", r=2)
        src3 = src.ap().rearrange("(k p) d -> p k d", p=128)
        for tc in range(NT):
            r = tc % 2
            if tc >= 2:
                P.wait("sync", self.sDV, self._stage_cast[r])
            self.dma("sync", xin3[:, r], src3[:, tc], sx)
            P.wait("vector", sx, sx.v)
            if tc >= 2:
                P.wait("vector", st, self._stage_tp[r])
            if quant:
                self._emit_quant(xin3[:, r], xbf3[:, r])
                cast_v = self.sDV.v
            else:
                cast_v = self.ve("tensor_copy", xbf3[:, r], xin3[:, r], sem=self.sDV)
            if not hasattr(self, "_stage_cast"):
                pass
            self._stage_cast[r] = cast_v
            P.wait("sync", self.sDV, cast_v)
            self.dmat(dst_view_fn(tc), xbf3[:, r], st)
            self._stage_tp[r] = st.v

    def begin_stage(self):
        self._stage_cast = [0, 0]
        self._stage_tp = [0, 0]

    def _emit_quant(self, xin_ap, out_bf):
        scr = self.scr
        s0 = scr[:, 0:1024]
        s1 = scr[:, 1024:2048]
        ts = self.ve
        ts("tensor_scalar", s0, xin_ap, self.sstat[:, 8:9], None, ALU.mult)
        ts("tensor_scalar", s0, s0, 448.0, -448.0, ALU.min, ALU.max)
        ts("tensor_mul", xin_ap, s0, s0)
        ts("tensor_scalar", s1, s0, 240.0, -240.0, ALU.min, ALU.max)
        ts("tensor_copy", self.xq8[:], s1)
        ts("tensor_copy", out_bf, self.xq8[:])
        ts("tensor_scalar", s1, s0, 24576.0, -24576.0, ALU.add, ALU.add)
        ts("tensor_single_scalar", self.msk[:], xin_ap, 0.000244140625, ALU.is_lt)
        ts("select", out_bf, self.msk[:], s1, out_bf)
        ts("tensor_scalar", s1, s0, 402653184.0, -402653184.0, ALU.add, ALU.add)
        ts("tensor_single_scalar", self.msk[:], xin_ap, 61504.0, ALU.is_ge)
        ts("select", out_bf, self.msk[:], s1, out_bf, sem=self.sDV)

    def resid_ln_store(self, src3, dst3, tc, add_ap, r):
        P = self.P
        xin3 = self.xin.ap().rearrange("p (q x) -> p q x", q=2)
        s0 = self.scr[:, 1024 * r:1024 * (r + 1)]
        st = self.sstat[:, 8 * r - 8 if False else 0:]
        c = 16 * 0
        o = 0 if r == 0 else 4
        stS = self.sstat[:, o + 0:o + 1]
        stQ = self.sstat[:, o + 1:o + 2]
        stM = self.sstat[:, o + 2:o + 3]
        stR = self.sstat[:, o + 3:o + 4]
        stV = self.sstat[:, 12 + r:13 + r]
        stM2 = self.sstat[:, 14 + r:15 + r]
        P.wait("sync", self.so, self._ln_store[r])
        self.dma("sync", xin3[:, r], src3[:, tc], self.sx)
        P.wait("vector", self.sx, self.sx.v)
        addv = self.ve("tensor_add", s0, xin3[:, r], add_ap, sem=self.sDV)
        self.ve("reduce_sum", stS, s0, axis=AX.X)
        self.ve("tensor_scalar", stM, stS, RECIP_D, None, ALU.mult)
        self.ve("tensor_mul", stM2, stM, stM)
        self.ve("tensor_mul", self.atrash[:], s0, s0)
        self.ve("reduce_sum", stQ, self.atrash[:], axis=AX.X)
        v = self.ve("scalar_tensor_tensor", stV, stQ, RECIP_D,
                    stM2, ALU.mult, ALU.subtract, sem=self.sDV)
        P.wait("scalar", self.sDV, v)
        self.act(stR, stV, AF.Ln, bias=self.sstat[:, 11:12])
        self.act(stR, stR, AF.Exp, scale=-0.5)
        a = self.act(self.sstat[:, 16 + r:17 + r], stR, AF.Identity, sem=self.sAC)
        P.wait("vector", self.sAC, a)
        v = self.ve("tensor_scalar", xin3[:, r], s0, stM, stR,
                    ALU.subtract, ALU.mult, sem=self.sDV)
        P.wait("sync", self.sDV, v)
        self.dma("sync", dst3[:, tc], xin3[:, r], self.so)
        self._ln_store[r] = self.so.v
        return addv

    def emit_attn(self, layer):
        P, dr = self.P, self.dram
        l0 = layer == 0
        src, dst = self.stream["l0a" if l0 else "l1a"]
        wnq = "wq" if l0 else "l1wq"
        wnk = "wk" if l0 else "l1wk"
        wnv = "wv" if l0 else "l1wv"
        bq_col, bk_col = (0, 8) if l0 else (16, 24)
        sx, st, sw, skv, sg, scc = self.sx, self.st, self.sw, self.skv, self.sg, self.scc
        sPE, sDV, sAC, sPL = self.sPE, self.sDV, self.sAC, self.sPL
        xfm3 = self.xfm.ap().rearrange("p (c t) -> p c t", c=8)
        src3 = src.ap().rearrange("(k p) d -> p k d", p=128)
        dst3 = dst.ap().rearrange("(k p) d -> p k d", p=128)
        wqk3 = self.wqk.ap().rearrange("p (r x) -> p r x", r=4)
        xbf = self.xbf

        if l0:
            self.dma("gpsimd", self.ropec[:], dr["ropeC"].ap(), sx)
            self.dma("gpsimd", self.ropes[:], dr["ropeS"].ap(), sx)
        else:
            self.dma("gpsimd", self.locmask[:], dr["locmask"].ap(), sx)
            self.dma("gpsimd", self.gmask[:], dr["gmask"].ap(), sx)
            self.dma("sync", self.gsel[:], dr["gsel"].ap(), sx)
            self.dma("sync", self.hsel[:], dr["hsel"].ap(), sx)
        consts_v = sx.v

        def reinit(eng):
            eng.memset(self.v65[0].ap().rearrange("p (c w) -> p c w", w=80)[:, :, 64:65], 1.0)
            eng.memset(self.v65[1].ap().rearrange("p (c w) -> p c w", w=80)[:, :, 64:65], 1.0)
            eng.memset(self.k65[0][64:65, :], 1.0)
            eng.memset(self.k65[1][64:65, :], 1.0)
            eng.memset(self.v65g[:, 64:65], 1.0)
            if not l0:
                eng.memset(self.wtile[:], 0.0)
                eng.memset(self.wrow[:], 0.0)
            eng.sem_inc(self.sPL.h, 1)
        P.on("vector", reinit)
        ones_v = self.sPL.inc(1)

        self.dma("gpsimd", self.bias8[:, bq_col:bq_col + 8],
                 dr["qb" if l0 else "l1bq"].ap().rearrange("(c p) -> p c", p=128),
                 sw, slow=True)
        self.dma("gpsimd", self.bias8[:, bk_col:bk_col + 8],
                 dr["kb" if l0 else "l1bk"].ap().rearrange("(c p) -> p c", p=128),
                 sw, slow=True)
        if not l0:
            self.dma("gpsimd", self.bias8[:, 32:40],
                     dr["l1bo"].ap().rearrange("(c p) -> p c", p=128), sw, slow=True)
        self.dma("sync", self.vrow[:], dr["vb" if l0 else "l1bv"].ap().unsqueeze(0), sx)
        vrow_v = sx.v
        self.dma("gpsimd", self.wfull[:],
                 dr[wnv].ap().rearrange("(c p) o -> p c o", p=128), sw)
        wv_v = sw.v

        self.begin_stage()
        self._cut(1)
        self.stage_fm(src, lambda tc: xfm3[:, :, 128 * tc:128 * (tc + 1)])
        stage_done = self.st.v
        self._cut(2)
        self._dbg("xfm", self.xfm.ap(), self.st, stage_done)

        def proj_side(wname, bias_col):
            pa, pb, pc = self.pa, self.pb, self.pc
            qp3 = self.qp.ap().rearrange("p (c t) -> p c t", c=8)
            last_consumer = [0]
            e16s = [0, 0]
            chunk_done = []
            for oc in range(8):
                self._cut(2.1 + oc / 100.0)
                r = oc % 2
                wsrc = dr[wname].ap().rearrange("(c p) o -> p c o", p=128)[:, :, 128 * oc:128 * (oc + 1)]
                if oc >= 2:
                    P.wait("gpsimd", sPE, self._wqk_used[r])
                wv_ = self.dma("gpsimd", wqk3[:, r], wsrc, sw)
                P.wait("tensor", st, stage_done)
                P.wait("tensor", sw, wv_)
                if oc > 0:
                    P.wait("tensor", sDV if l0 else sAC, last_consumer[0])
                for cc in range(8):
                    lhs = wqk3[:, r][:, 128 * cc:128 * (cc + 1)]
                    for tb in range(4):
                        ps = (pa, pa, pb, pb)[tb]
                        col = (0, 512, 0, 512)[tb]
                        sem = sPE if (cc == 7 and tb == 3) else None
                        self.mm(ps[:, col:col + 512],
                                lhs, xfm3[:, cc][:, 512 * tb:512 * (tb + 1)],
                                start=(cc == 0), stop=(cc == 7), sem=sem)
                proj_v = sPE.v
                self._wqk_used[r] = proj_v
                P.wait("scalar", sPE, proj_v)
                P.wait("scalar", st, stage_done)
                bcol = self.bias8[:, bias_col + oc:bias_col + oc + 1]
                if l0:
                    self.act(xbf[:, 0:1024], pa[:], AF.Identity, bias=bcol)
                    ev = self.act(xbf[:, 1024:2048], pb[:], AF.Identity, bias=bcol, sem=sAC)
                    P.wait("tensor", sAC, ev)
                    P.wait("tensor", sPL, 1)
                    for tb in range(4):
                        ps = (pa, pa, pb, pb)[tb]
                        col = (0, 512, 0, 512)[tb]
                        self.mm(ps[:, col:col + 512], self.mswap[:],
                                xbf[:, 512 * tb:512 * (tb + 1)],
                                start=True, stop=True,
                                sem=sPE if tb == 3 else None)
                    sw_v = sPE.v
                    P.wait("vector", sAC, ev)
                    self.ve("tensor_mul", self.scr[:], xbf[:], self.ropec[:])
                    P.wait("vector", sPE, sw_v)
                    self.ve("tensor_mul", qp3[:, oc][:, 0:1024], pa[:], self.ropes[:, 0:1024])
                    v = self.ve("tensor_mul", qp3[:, oc][:, 1024:2048], pb[:],
                                self.ropes[:, 1024:2048], sem=sDV)
                    last_consumer[0] = v
                    v = self.ve("tensor_add", qp3[:, oc], qp3[:, oc], self.scr[:], sem=sDV)
                    cd_sem, cd = sDV, v
                else:
                    self.act(qp3[:, oc][:, 0:1024], pa[:], AF.Identity, bias=bcol)
                    ev = self.act(qp3[:, oc][:, 1024:2048], pb[:], AF.Identity, bias=bcol, sem=sAC)
                    last_consumer[0] = ev
                    cd_sem, cd = sAC, ev
                chunk_done.append((cd_sem, cd))
                self._cut(2.5 + oc / 100.0)
                P.wait("scalar", cd_sem, cd)
                if l0:
                    P.wait("scalar", sPE, sw_v)
                sq = self.act(xbf[:], qp3[:, oc], AF.Square, sem=sAC)
                P.wait("tensor", sAC, sq)
                e16l = self.e16[:, 16 * oc:16 * (oc + 1)]
                for half in range(2):
                    P.wait("tensor", sDV, max(e16s))
                    self.mm(pc[0:16, 0:512], e16l, xbf[:, 1024 * half:1024 * half + 512],
                            start=True, stop=True)
                    v = self.mm(pc[0:16, 512:1024], e16l,
                                xbf[:, 1024 * half + 512:1024 * (half + 1)],
                                start=True, stop=True, sem=sPE)
                    P.wait("vector", sPE, v)
                    if oc == 0 and half == 0:
                        pass
                    if oc == 0:
                        v = self.ve("tensor_copy", self.q2sb[0:16, 1024 * half:1024 * (half + 1)],
                                    pc[0:16, :], sem=sDV)
                    else:
                        v = self.ve("tensor_add", self.q2sb[0:16, 1024 * half:1024 * (half + 1)],
                                    self.q2sb[0:16, 1024 * half:1024 * (half + 1)],
                                    pc[0:16, :], sem=sDV)
                    e16s[half] = v
            return chunk_done, e16s[1], sq

        self._wqk_used = [0, 0]
        k_chunks, k_q2_done, k_sq_last = proj_side(wnk, bk_col)
        self._cut(3)
        self._dbg("kchunks", self.qp.ap(), self.sDV if l0 else self.sAC)
        self._dbg("k2", self.q2sb[0:16, :], self.sDV, k_q2_done)
        qp3 = self.qp.ap().rearrange("p (c t) -> p c t", c=8)
        for oc in range(8):
            cs, cv = k_chunks[oc]
            P.wait("sync", cs, cv)
            self.dma("sync", dr["kfm_own"].ap()[128 * oc:128 * (oc + 1), :], qp3[:, oc], skv)
        kfm_done = skv.v
        P.wait("vector", self.sDV, k_q2_done)
        v = self.ve("reduce_max", self.kmx[:, 0:1], self.q2sb[0:16, :], axis=AX.X, sem=sDV)
        P.wait("sync", sDV, v)
        self.dma("sync", dr["kmx_own"].ap(), self.kmx[0:16, 0:1], skv)
        kmx_dma = skv.v
        self._cut(3.5)

        def coll(eng, ins, outs, s=scc.h):
            eng.collective_compute(
                "AllGather", ALU.bypass,
                replica_groups=[[0, 1], [2, 3], [4, 5], [6, 7]],
                ins=[ins], outs=[outs],
            ).then_inc(s)

        P.wait("gpsimd", skv, kfm_done)
        P.on("gpsimd", lambda eng: coll(eng, dr["kfm_own"].ap().opt(), dr["kfm_all"].ap().opt()))
        scc.inc(1)
        cc_k = scc.v
        self._cut(3.7)
        P.wait("gpsimd", skv, kmx_dma)
        P.on("gpsimd", lambda eng: coll(eng, dr["kmx_own"].ap().opt(), dr["kmx_all"].ap().opt()))
        scc.inc(1)
        cc_km = scc.v
        self._cut(4)

        P.wait("tensor", sx, vrow_v)
        P.wait("tensor", sPL, 1)
        P.wait("tensor", sDV, k_q2_done)
        self.mm(self.pc[:, 0:512], self.ones_r[0:1, 0:128], self.vrow[0:1, 0:512])
        v = self.mm(self.pc[:, 512:1024], self.ones_r[0:1, 0:128],
                    self.vrow[0:1, 512:1024], sem=sPE)
        P.wait("vector", sPE, v)
        vb_v = self.ve("tensor_copy", self.scr[:, 0:1024], self.pc[:], sem=sDV)

        vst = [0, 0]
        evv = [0, 0]
        xbf2 = xbf.ap().rearrange("p (r x) -> p r x", r=2)
        vhm4 = dr["vhm_own"].ap().rearrange("(h k p) d -> p k h d", p=128, k=16)
        for tc in range(NT):
            self._cut(4.1 + tc / 100.0)
            r = tc % 2
            pp = self.pa if r == 0 else self.pb
            if tc < 2:
                P.wait("tensor", sw, wv_v)
                P.wait("tensor", sDV, vb_v)
            P.wait("tensor", sDV, evv[r])
            for cc in range(8):
                lhs = xfm3[:, cc][:, 128 * tc:128 * (tc + 1)]
                self.mm(pp[:, 0:512], lhs, self.wfull[:, 1024 * cc:1024 * cc + 512],
                        start=(cc == 0), stop=(cc == 7))
                self.mm(pp[:, 512:1024], lhs, self.wfull[:, 1024 * cc + 512:1024 * (cc + 1)],
                        start=(cc == 0), stop=(cc == 7),
                        sem=sPE if cc == 7 else None)
            pv_ = sPE.v
            P.wait("vector", sPE, pv_)
            P.wait("vector", skv, max(vst))
            if tc < 2:
                P.wait("vector", self.sAC, k_sq_last)
            ev = self.ve("tensor_add", xbf2[:, r], pp[:], self.scr[:, 0:1024], sem=sDV)
            evv[r] = ev
            P.wait("sync", sDV, ev)
            self.dma("sync", vhm4[:, tc],
                     xbf.ap().rearrange("p (r h d) -> p r h d", r=2, h=16)[:, r], skv)
            vst[r] = skv.v
        vhm_done = skv.v
        P.wait("gpsimd", skv, vhm_done)
        P.on("gpsimd", lambda eng: coll(eng, dr["vhm_own"].ap().opt(), dr["vhm_all"].ap().opt()))
        scc.inc(1)
        cc_v = scc.v
        self._cut(5)

        P.wait("scalar", skv, vhm_done)
        P.wait("vector", skv, vhm_done)
        self._dbg("bias8", self.bias8[:], self.sw)
        q_chunks, q_q2_done, _ = proj_side(wnq, bq_col)
        self._dbg("qpsum", self.qp.ap(), self.sDV if l0 else self.sAC)

        P.wait("sync", scc, cc_km)
        self.dma("sync", self.kmx[:, 1:2], dr["kmx_all"].ap()[0:16, :], skv)
        v = self.dma("sync", self.kmx[:, 2:3], dr["kmx_all"].ap()[16:32, :], skv)
        P.wait("vector", skv, v)
        self.ve("tensor_max", self.kmx[:, 3:4], self.kmx[:, 1:2], self.kmx[:, 2:3])
        self.ve("tensor_max", self.kmx[:, 3:4], self.kmx[:, 3:4], self.kmx[:, 0:1])
        P.wait("vector", sDV, q_q2_done)
        v = self.ve("tensor_scalar", self.q2sb[0:16, :], self.q2sb[0:16, :],
                    self.kmx[0:16, 3:4], None, ALU.mult, sem=sDV)
        P.wait("scalar", sDV, v)
        self.act(self.scr[0:16, :], self.q2sb[0:16, :], AF.Ln, bias=self.sstat[0:16, 10:11])
        a = self.act(self.q2sb[0:16, :], self.scr[0:16, :], AF.Exp, scale=0.5, sem=sAC)
        P.wait("vector", sAC, a)
        gv = self.ve("tensor_scalar", self.gall[0:16, :], self.q2sb[0:16, :],
                     -1.0, None, ALU.mult, sem=sDV)
        self._cut(6)
        self._dbg("qchunks", self.qp.ap(), self.sDV)
        self._dbg("gall", self.gall[0:16, :], self.sDV, gv)
        self._dbg("kfm", dr["kfm_all"].ap()[0:1024, :], self.scc, cc_k)
        self._dbg("vhm", dr["vhm_all"].ap()[0:16384, :], self.scc, cc_v)
        self._dbg("kmx", self.kmx[0:16, :], self.sDV)

        if l0:
            app_v = self._head_loop_l0(cc_k, cc_v, q_chunks, gv)
            atm_t, atm_src = self.atm, self.xfm
        else:
            app_v = self._head_loop_l1(cc_k, cc_v, q_chunks, gv)
            self._dbg("attnpre", self.xfm.ap(), self.sDV, app_v)
            app_v = self._oproj(app_v)
            atm_t, atm_src = self.xfm, self.qp
        self._cut(7)
        self._dbg("attnfm", self.xfm.ap() if l0 else self.qp.ap(), self.sDV if l0 else self.sAC)
        self._dbg("recip", self.recip[0:1, :], self.sDV)
        self._dbg("vhmo", dr["vhm_own"].ap()[30720:32768, :], self.scc)
        self._dbg("vhma", dr["vhm_all"].ap()[63488:65536, :], self.scc)
        self._dbg("v65", self.v65[1].ap(), self.sDV)
        self._dbg("k65", self.k65[1].ap(), self.sDV)
        self._dbg("q65", self.q65[1].ap(), self.sDV)
        af3s = atm_src.ap().rearrange("p (c t) -> p c t", c=8)
        atm3 = atm_t.ap().rearrange("p (k d) -> p k d", k=16)
        P.wait("sync", sDV if l0 else sAC, app_v)
        for c in range(8):
            self.dmat(atm3[:, :, 128 * c:128 * (c + 1)], af3s[:, c], st)
        tp_done = st.v
        P.wait("vector", st, tp_done)
        self._ln_store = [0, 0]
        for tc in range(NT):
            self.resid_ln_store(src3, dst3, tc,
                                atm_t[:, 1024 * tc:1024 * (tc + 1)], tc % 2)

    def _head_loop_l0(self, cc_k, cc_v, q_chunks, gv):
        P, dr = self.P, self.dram
        sPE, sDV, sAC, skv, sg, scc = self.sPE, self.sDV, self.sAC, self.skv, self.sg, self.scc
        af3 = self.xfm.ap().rearrange("p (c t) -> p c t", c=8)
        qp3 = self.qp.ap().rearrange("p (c t) -> p c t", c=8)
        kall, vall = self.dram["kfm_all"], self.dram["vhm_all"]
        vall5 = vall.ap().rearrange("(r h k p) d -> r h p k d", p=128, k=16, h=16)
        scores_end = {}
        pv_end = {}
        exp_par = [0, 0]
        evac_r_v = 0
        rapply_v = 0
        kv_done = {}
        qb_done = {}
        for h in range(16):
            self._cut(6.1 + h / 100.0)
            r = h % 2
            if h >= 2:
                P.wait("sync", sPE, scores_end[h - 2])
                P.wait("sync", sPE, pv_end[h - 2])
            else:
                P.wait("sync", scc, cc_k)
                P.wait("sync", scc, cc_v)
            self.dma("sync", self.k65[r][0:64, 0:2048], kall.ap()[64 * h:64 * h + 64, :], skv)
            self.dma("sync", self.k65[r][0:64, 2048:4096],
                     kall.ap()[1024 + 64 * h:1024 + 64 * h + 64, :], skv)
            v65v = self.v65[r].ap().rearrange("p (c w) -> p c w", w=80)
            self.dma("sync", v65v[:, 0:16, 0:64], vall5[0, h], skv)
            kvv = self.dma("sync", v65v[:, 16:32, 0:64], vall5[1, h], skv)
            kv_done[h] = kvv
            cs, cv = q_chunks[h // 2]
            P.wait("vector", cs, cv)
            if h >= 2:
                P.wait("vector", sPE, scores_end[h - 2])
            qv = self.ve("tensor_copy", self.q65[r][0:64, :],
                         qp3[:, h // 2][64 * (h % 2):64 * (h % 2) + 64, :], sem=sDV)
            P.wait("gpsimd", sDV, gv)
            if h >= 2:
                P.wait("gpsimd", sPE, scores_end[h - 2])
            grv = self.dma("gpsimd", self.q65[r][64:65, :], self.gall[h:h + 1, :], sg)
            qb_done[h] = (qv, grv)

            for p_ in range(2):
                self._cut(6.5 + h / 100.0 + p_ / 1000.0)
                sv = pvv = 0
                pv_prev = [0]

                def emit_pv(kc, pvv_sem=None):
                    par = kc % 2
                    P.wait("tensor", sAC, exp_par[par])
                    if kc == 0:
                        P.wait("tensor", sDV, evac_r_v)
                    vl = self.v65[r][:, 80 * kc:80 * kc + 65]
                    self.mm(self.pc[0:65, 0:512], vl, self.pt[par][:, 0:512],
                            start=(kc == 0), stop=(kc == 31))
                    return self.mm(self.pc[0:65, 512:1024], vl, self.pt[par][:, 512:1024],
                                   start=(kc == 0), stop=(kc == 31), sem=sPE)

                for kc in range(32):
                    par = kc % 2
                    sc = self.pa if par == 0 else self.pb
                    if kc < 2 and p_ == 0:
                        P.wait("tensor", skv, kv_done[h])
                        P.wait("tensor", sDV, qb_done[h][0])
                        P.wait("tensor", sg, qb_done[h][1])
                    P.wait("tensor", sAC, exp_par[par])
                    lhs = self.k65[r][0:65, 128 * kc:128 * (kc + 1)]
                    self.mm(sc[:, 0:512], lhs, self.q65[r][0:65, 1024 * p_:1024 * p_ + 512])
                    sv = self.mm(sc[:, 512:1024], lhs,
                                 self.q65[r][0:65, 1024 * p_ + 512:1024 * (p_ + 1)], sem=sPE)
                    if kc > 0:
                        pv_prev[0] = emit_pv(kc - 1)
                    P.wait("scalar", sPE, sv)
                    ev = self.act(self.pt[par][:], sc[:], AF.Exp, scale=0.125, sem=sAC)
                    exp_par[par] = ev
                pvv = emit_pv(31)
                self._cut(6.7 + h / 100.0 + p_ / 1000.0)
                P.wait("vector", sPE, pvv)
                dsts = []
                for qb in range(2):
                    col = 512 * qb
                    dst = af3[:, h // 2][64 * (h % 2):64 * (h % 2) + 64,
                                         1024 * p_ + col:1024 * p_ + col + 512]
                    dsts.append(dst)
                    self.ve("tensor_copy", dst, self.pc[0:64, col:col + 512])
                    rv = self.ve("reciprocal",
                                 self.recip[0:1, 1024 * p_ + col:1024 * p_ + col + 512],
                                 self.pc[64:65, col:col + 512], sem=sDV)
                evac_r_v = rv
                for qb in range(2):
                    P.wait("tensor", sDV, rv if qb == 0 else rapply_v)
                    rmv = self.mm(self.pd[0:64, 0:512], self.ones_r[0:1, 0:64],
                                  self.recip[0:1, 1024 * p_ + 512 * qb:1024 * p_ + 512 * qb + 512],
                                  sem=sPE)
                    P.wait("vector", sPE, rmv)
                    rapply_v = self.ve("tensor_mul", dsts[qb], dsts[qb],
                                       self.pd[0:64, 0:512], sem=sDV)
            scores_end[h] = sv
            pv_end[h] = pvv
        return rapply_v

    def _head_loop_l1(self, cc_k, cc_v, q_chunks, gv):
        P, dr = self.P, self.dram
        sPE, sDV, sAC, sPL, skv, sg, scc = (self.sPE, self.sDV, self.sAC,
                                            self.sPL, self.skv, self.sg, self.scc)
        af3 = self.xfm.ap().rearrange("p (c t) -> p c t", c=8)
        qp3 = self.qp.ap().rearrange("p (c t) -> p c t", c=8)
        kall, vall = dr["kfm_all"], dr["vhm_all"]
        vall5 = vall.ap().rearrange("(r h k p) d -> r h p k d", p=128, k=16, h=16)
        vflat = vall.ap()
        head_end = {}
        exp_par = [0, 0]
        evac_v = {}
        rapply_v = 0
        mini_exp_v = 0
        kv_done = {}
        qb_done = {}
        merge_v = 0
        for h in range(16):
            self._cut(6.1 + h / 100.0)
            r = h % 2
            if h >= 2:
                P.wait("sync", sPE, head_end[h - 2])
            else:
                P.wait("sync", scc, cc_k)
                P.wait("sync", scc, cc_v)
            self.dma("sync", self.k65[r][0:64, 0:2048], kall.ap()[64 * h:64 * h + 64, :], skv)
            self.dma("sync", self.k65[r][0:64, 2048:4096],
                     kall.ap()[1024 + 64 * h:1024 + 64 * h + 64, :], skv)
            v65v = self.v65[r].ap().rearrange("p (c w) -> p c w", w=80)
            self.dma("sync", v65v[:, 0:16, 0:64], vall5[0, h], skv)
            kvv = self.dma("sync", v65v[:, 16:32, 0:64], vall5[1, h], skv)
            if h >= 1:
                P.wait("sync", sPE, head_end[h - 1])
            g4a = bass.AP(dr["vhm_all"], (2048 * h) * 64, [[585 * 64, 4], [1, 64]])
            g4b = bass.AP(dr["vhm_all"], (32768 + 2048 * h + 292) * 64, [[585 * 64, 4], [1, 64]])
            self.dma("sync", self.v65g[0:4, 0:64], g4a, skv)
            kvv = self.dma("sync", self.v65g[4:8, 0:64], g4b, skv)
            kv_done[h] = kvv
            cs, cv = q_chunks[h // 2]
            P.wait("vector", cs, cv)
            if h >= 2:
                P.wait("vector", sPE, head_end[h - 2])
            qv = self.ve("tensor_copy", self.q65[r][0:64, :],
                         qp3[:, h // 2][64 * (h % 2):64 * (h % 2) + 64, :], sem=sDV)
            P.wait("gpsimd", sDV, gv)
            if h >= 2:
                P.wait("gpsimd", sPE, head_end[h - 2])
            grv = self.dma("gpsimd", self.q65[r][64:65, :], self.gall[h:h + 1, :], sg)
            P.wait("vector", skv, kv_done[h])
            P.wait("vector", sg, grv)
            if h >= 1:
                P.wait("vector", sPE, head_end[h - 1])
            self.ve("select", self.k65loc[0:65, :], self.hsel[0:65, :],
                    self.k65[r][0:65, 2048:4096], self.k65[r][0:65, 0:2048])
            self.ve("select", self.v65loc[:, :], self.hsel[:, 0:1280],
                    self.v65[r][:, 1280:2560], self.v65[r][:, 0:1280])
            self.ve("tensor_copy", self.k65g[0:65, 0:8], self.k65[r][0:65, 0:4096:585])
            self.ve("tensor_copy", self.q65g[0:65, 0:4], self.q65[r][0:65, 0:2048:585])
            qgv = self.ve("tensor_copy", self.q65g[0:65, 4:8],
                          self.q65[r][0:65, 292:2048:585], sem=sDV)
            qb_done[h] = (qv, grv, qgv)
            P.wait("tensor", sDV, qgv)

            mask_v = {}

            def l1_pv(qtg):
                par = qtg % 2
                col = 512 * par
                P.wait("tensor", sPL, mask_v[qtg])
                if qtg >= 2:
                    P.wait("tensor", sDV, evac_v[qtg - 2])
                for j in range(4):
                    qt = 4 * qtg + j
                    self.mm(self.pc[0:65, col + 128 * j:col + 128 * (j + 1)],
                            self.v65loc[:, 80 * qt:80 * qt + 65],
                            self.pt[par][:, 128 * j:128 * (j + 1)],
                            start=(j == 0), stop=False)
                pvv = self.mm(self.pc[0:65, col:col + 512], self.v65g[0:8, 0:65],
                              self.pt[par][0:8, 512:1024],
                              start=False, stop=True, sem=sPE)
                P.wait("vector", sPE, pvv)
                dst = af3[:, h // 2][64 * (h % 2):64 * (h % 2) + 64,
                                     512 * qtg:512 * (qtg + 1)]
                self.ve("tensor_copy", dst, self.pc[0:64, col:col + 512])
                cvv = self.ve("tensor_copy",
                              self.recip[0:1, 512 * qtg:512 * (qtg + 1)],
                              self.pc[64:65, col:col + 512], sem=sDV)
                evac_v[qtg] = cvv
                return pvv

            for qtg in range(4):
                self._cut(6.2 + h / 100.0 + qtg / 1000.0)
                par = qtg % 2
                col = 512 * par
                if qtg < 2:
                    P.wait("tensor", skv, kv_done[h])
                    P.wait("tensor", sDV, qb_done[h][2])
                    P.wait("tensor", sg, grv)
                    P.wait("tensor", sDV, merge_v)
                P.wait("tensor", sAC, exp_par[par])
                for j in range(4):
                    qt = 4 * qtg + j
                    self.mm(self.pa[:, col + 128 * j:col + 128 * (j + 1)],
                            self.k65loc[0:65, 128 * qt:128 * (qt + 1)],
                            self.q65[r][0:65, 128 * qt:128 * (qt + 1)])
                sv = self.mm(self.pb[0:8, col:col + 512], self.k65g[0:65, 0:8],
                             self.q65[r][0:65, 512 * qtg:512 * (qtg + 1)], sem=sPE)
                if qtg > 0:
                    pvv = l1_pv(qtg - 1)
                P.wait("scalar", sPE, sv)
                self.act(self.pt[par][:, 0:512], self.pa[:, col:col + 512],
                         AF.Exp, scale=0.125)
                ev = self.act(self.pt[par][0:8, 512:1024], self.pb[0:8, col:col + 512],
                              AF.Exp, scale=0.125, sem=sAC)
                exp_par[par] = ev
                P.wait("gpsimd", sAC, ev)
                self.ve("tensor_mul", self.pt[par][:, 0:512], self.pt[par][:, 0:512],
                        self.locmask[:], eng_name="gpsimd")
                mv = self.ve("tensor_mul", self.pt[par][0:8, 512:1024],
                             self.pt[par][0:8, 512:1024],
                             self.gmask[0:8, 512 * qtg:512 * (qtg + 1)],
                             sem=sPL, eng_name="gpsimd")
                mask_v[qtg] = mv
            pvv = l1_pv(3)

            self._cut(6.5 + h / 100.0)
            P.wait("tensor", sAC, mini_exp_v)
            P.wait("tensor", sDV, rapply_v)
            P.wait("tensor", sDV, qb_done[h][2])
            for kc in range(32):
                if kc % 8 == 0 and kc > 0:
                    P.wait("tensor", sAC, self.sAC.v)
                self.mm(self.pd[:, 8 * kc:8 * (kc + 1)], self.k65[r][0:65, 128 * kc:128 * (kc + 1)],
                        self.q65g[0:65, 0:8],
                        sem=sPE if kc % 8 == 7 else None)
                if kc % 8 == 7:
                    g = kc // 8
                    P.wait("scalar", sPE, self.sPE.v)
                    ae = self.act(self.ptm[:, 64 * g:64 * (g + 1)],
                                  self.pd[:, 64 * g:64 * (g + 1)],
                                  AF.Exp, scale=0.125, sem=sAC)
            mini_exp_v = ae
            P.wait("tensor", sDV, evac_v[2])
            P.wait("tensor", sDV, evac_v[3])
            for kc in range(32):
                if kc % 8 == 0:
                    P.wait("tensor", sAC, mini_exp_v - 3 + kc // 8)
                ogv = self.mm(self.pc[0:65, 0:8], self.v65[r][:, 80 * kc:80 * kc + 65],
                              self.ptm[:, 8 * kc:8 * (kc + 1)],
                              start=(kc == 0), stop=(kc == 31),
                              sem=sPE if kc == 31 else None)
            self._cut(6.6 + h / 100.0)
            P.wait("vector", sPE, ogv)
            self.ve("tensor_copy", self.wtile[0:64, 0:2048:585], self.pc[0:64, 0:4])
            self.ve("tensor_copy", self.wtile[0:64, 292:2048:585], self.pc[0:64, 4:8])
            self.ve("tensor_copy", self.wrow[0:1, 0:2048:585], self.pc[64:65, 0:4])
            merge_v = self.ve("tensor_copy", self.wrow[0:1, 292:2048:585],
                              self.pc[64:65, 4:8], sem=sDV)
            dsta = af3[:, h // 2][64 * (h % 2):64 * (h % 2) + 64, :]
            self.ve("select", dsta, self.gsel[0:64, :], self.wtile[0:64, :], dsta)
            self.ve("select", self.recip[0:1, :], self.gsel[0:1, :],
                    self.wrow[0:1, :], self.recip[0:1, :])
            rv = self.ve("reciprocal", self.scr[0:1, 0:2048], self.recip[0:1, :], sem=sDV)
            self._cut(6.7 + h / 100.0)
            for qb in range(4):
                P.wait("tensor", sDV, rv if qb == 0 else rapply_v)
                P.wait("tensor", sAC, mini_exp_v)
                rmv = self.mm(self.pd[0:64, 0:512], self.ones_r[0:1, 0:64],
                              self.scr[0:1, 512 * qb:512 * (qb + 1)], sem=sPE)
                P.wait("vector", sPE, rmv)
                dst = af3[:, h // 2][64 * (h % 2):64 * (h % 2) + 64,
                                     512 * qb:512 * (qb + 1)]
                rapply_v = self.ve("tensor_mul", dst, dst, self.pd[0:64, 0:512], sem=sDV)
            head_end[h] = rmv
        return rapply_v

    def _oproj(self, app_v):
        P, dr = self.P, self.dram
        sPE, sDV, sAC, sw = self.sPE, self.sDV, self.sAC, self.sw
        af3 = self.xfm.ap().rearrange("p (c t) -> p c t", c=8)
        op3 = self.qp.ap().rearrange("p (c t) -> p c t", c=8)
        wqk3 = self.wqk.ap().rearrange("p (r x) -> p r x", r=4)
        ev_prev = 0
        for oc in range(8):
            r = oc % 2
            wsrc = dr["l1wo"].ap().rearrange("(c p) o -> p c o", p=128)[:, :, 128 * oc:128 * (oc + 1)]
            if oc >= 2:
                P.wait("gpsimd", sPE, self._wqk_used[r])
            wv_ = self.dma("gpsimd", wqk3[:, r], wsrc, sw)
            P.wait("tensor", sw, wv_)
            if oc == 0:
                P.wait("tensor", sDV, app_v)
            else:
                P.wait("tensor", sAC, ev_prev)
            for cc in range(8):
                lhs = wqk3[:, r][:, 128 * cc:128 * (cc + 1)]
                for tb in range(4):
                    ps = (self.pa, self.pa, self.pb, self.pb)[tb]
                    col = (0, 512, 0, 512)[tb]
                    self.mm(ps[:, col:col + 512], lhs,
                            af3[:, cc][:, 512 * tb:512 * (tb + 1)],
                            start=(cc == 0), stop=(cc == 7),
                            sem=sPE if (cc == 7 and tb == 3) else None)
            pv_ = sPE.v
            self._wqk_used[r] = pv_
            P.wait("scalar", sPE, pv_)
            bcol = self.bias8[:, 32 + oc:33 + oc]
            self.act(op3[:, oc][:, 0:1024], self.pa[:], AF.Identity, bias=bcol)
            ev_prev = self.act(op3[:, oc][:, 1024:2048], self.pb[:], AF.Identity,
                               bias=bcol, sem=sAC)
        return ev_prev

    def emit_moe(self):
        P, dr = self.P, self.dram
        src, dst = self.stream["moe"]
        sx, st, sw, sg = self.sx, self.st, self.sw, self.sg
        sPE, sDV, sAC = self.sPE, self.sDV, self.sAC
        src3 = src.ap().rearrange("(k p) d -> p k d", p=128)
        dst3 = dst.ap().rearrange("(k p) d -> p k d", p=128)
        xm4 = self.xfm.ap().rearrange("p (s c t) -> p s c t", s=2, c=8)
        wqk3 = self.wqk.ap().rearrange("p (r x) -> p r x", r=4)
        xbf2 = self.xbf.ap().rearrange("p (r x) -> p r x", r=2)

        self.dma("gpsimd", self.rwsb.ap().rearrange("p (c e) -> p c e", c=8),
                 dr["rw"].ap().rearrange("(c p) e -> p c e", p=128), sw, slow=True)
        for e in range(E):
            self.dma("gpsimd", self.gub[:, 22 * e:22 * e + 21],
                     dr["gb"].ap()[e:e + 1, 0:2688].rearrange("a (c p) -> p a c", p=128).squeeze(1),
                     sw, slow=True)
            self.dma("gpsimd", self.gub[0:42, 22 * e + 21:22 * e + 22],
                     dr["gb"].ap()[e:e + 1, 2688:2730].rearrange("a p -> p a"),
                     sw, slow=True)
            self.dma("gpsimd", self.gub[:, 88 + 22 * e:88 + 22 * e + 21],
                     dr["ub"].ap()[e:e + 1, 0:2688].rearrange("a (c p) -> p a c", p=128).squeeze(1),
                     sw, slow=True)
            self.dma("gpsimd", self.gub[0:42, 88 + 22 * e + 21:88 + 22 * e + 22],
                     dr["ub"].ap()[e:e + 1, 2688:2730].rearrange("a p -> p a"),
                     sw, slow=True)
        cst_v = self.dma("gpsimd", self.db4[:], dr["db"].ap(), sw)

        self.begin_stage()
        self.stage_fm(src, lambda tc: xm4[:, tc // 8, :, 128 * (tc % 8):128 * (tc % 8 + 1)])
        stage_done = self.st.v

        top2_v = [0, 0]
        for tc in range(NT):
            r = tc % 2
            pp = self.pa if r == 0 else self.pb
            P.wait("tensor", st, stage_done)
            P.wait("tensor", sw, cst_v)
            P.wait("tensor", sDV, top2_v[r])
            for cc in range(8):
                self.mm(pp[:, 0:4], xm4[:, tc // 8, cc][:, 128 * (tc % 8):128 * (tc % 8 + 1)],
                        self.rwsb[:, 4 * cc:4 * (cc + 1)],
                        start=(cc == 0), stop=(cc == 7),
                        sem=sPE if cc == 7 else None)
            rv = self.sPE.v
            lg = self.logi
            L = pp[:, 0:4]
            P.wait("vector", sPE, rv)
            self.ve("reduce_max", lg[:, 0:1], L, axis=AX.X)
            self.ve("tensor_scalar", lg[:, 4:8], L, lg[:, 0:1], None, ALU.is_equal)
            self.ve("scalar_tensor_tensor", lg[:, 8:12], lg[:, 4:8], -1.0e30, L,
                    ALU.mult, ALU.add)
            self.ve("reduce_max", lg[:, 1:2], lg[:, 8:12], axis=AX.X)
            v = self.ve("tensor_scalar", lg[:, 2:3], lg[:, 0:1], -1.0, None,
                        ALU.mult, sem=sDV)
            P.wait("scalar", sDV, v)
            self.act(lg[:, 4:8], L, AF.Exp, bias=lg[:, 2:3])
            self.act(lg[:, 3:4], lg[:, 1:2], AF.Exp, bias=lg[:, 2:3])
            a = self.act(self.atrash[:, 0:5], lg[:, 3:8], AF.Identity, sem=sAC)
            P.wait("vector", sAC, a)
            self.ve("tensor_scalar", lg[:, 3:4], lg[:, 3:4], 1.0, None, ALU.add)
            self.ve("reciprocal", lg[:, 12:13], lg[:, 3:4])
            self.ve("tensor_scalar", lg[:, 8:12], L, lg[:, 1:2], None, ALU.is_ge)
            self.ve("tensor_scalar", self.wts[:, 4 * tc:4 * (tc + 1)], lg[:, 4:8],
                    lg[:, 12:13], None, ALU.mult)
            top2_v[r] = self.ve("tensor_mul", self.wts[:, 4 * tc:4 * (tc + 1)],
                                self.wts[:, 4 * tc:4 * (tc + 1)], lg[:, 8:12], sem=sDV)

        P.wait("vector", sDV, top2_v[0])
        P.wait("vector", sDV, top2_v[1])
        wb_v = self.ve("tensor_copy", self.wtsb[:], self.wts[:], sem=sDV)
        P.wait("tensor", sDV, wb_v)
        P.wait("tensor", self.sg, 1)
        for half in range(2):
            for j in range(8):
                tc = 8 * half + j
                def tr(eng, o=self.pe[0:4, 128 * j:128 * (j + 1)],
                       i=self.wtsb[:, 4 * tc:4 * (tc + 1)], idt=self.ident[:],
                       s=self.sPE.h, last=(j == 7)):
                    ins = eng.transpose(o, i, idt)
                    if last:
                        ins.then_inc(s, 1)
                P.on("tensor", tr)
            tv = self.sPE.inc()
            P.wait("vector", sPE, tv)
            ev = self.ve("tensor_copy", self.wtst[0:4, 1024 * half:1024 * (half + 1)],
                         self.pe[0:4, :], sem=sDV)
            if half == 0:
                P.wait("tensor", sDV, ev)
        wtst_v = ev

        hid3 = self.hid.ap().rearrange("p (c t) -> p c t", c=NHC)
        wd_end = 0
        down_ev = [0, 0]
        resid_v = 0
        gu_used4 = [0, 0, 0, 0]
        for th in range(2):
            for e in range(E):
                P.wait("gpsimd", sPE, wd_end)
                self.dma("gpsimd", self.wdt.ap()[:, 0:21504].rearrange("p (c o) -> p c o", c=21),
                         dr["dw"].ap()[e, 0:2688, :].rearrange("(c p) o -> p c o", p=128), sw)
                wd_v = self.dma("gpsimd", self.wdt[0:42, 21504:22528],
                                dr["dw"].ap()[e, 2688:2730, :], sw)
                silu_v = {}
                for hc in range(NHC):
                    hsz = HCS[hc]
                    r = hc % 2
                    r4 = hc % 4
                    pp = self.pa if r == 0 else self.pb
                    P.wait("gpsimd", sPE, gu_used4[r4])
                    wv_ = self.dma("gpsimd",
                                   self.wqk.ap()[:, 1024 * r4:1024 * r4 + 8 * hsz].rearrange("p (c h) -> p c h", c=8),
                                   dr["gw"].ap()[e, :, 128 * hc:128 * hc + hsz].rearrange("(c p) h -> p c h", p=128),
                                   sw)
                    P.wait("tensor", sw, wv_)
                    P.wait("tensor", st, stage_done)
                    P.wait("tensor", sAC, silu_v.get(hc - 2, 0))
                    for cc in range(8):
                        lhs = self.wqk.ap()[:, 1024 * r4 + hsz * cc:1024 * r4 + hsz * (cc + 1)]
                        self.mm(pp[0:hsz, 0:512], lhs, xm4[:, th, cc][:, 0:512],
                                start=(cc == 0), stop=(cc == 7))
                        self.mm(pp[0:hsz, 512:1024], lhs, xm4[:, th, cc][:, 512:1024],
                                start=(cc == 0), stop=(cc == 7),
                                sem=sPE if cc == 7 else None)
                    gu_used4[r4] = self.sPE.v
                    P.wait("scalar", sPE, self.sPE.v)
                    silu_v[hc] = self.act(hid3[:, hc][0:hsz, :], pp[0:hsz, :], AF.Silu,
                                          bias=self.gub[0:hsz, 22 * e + hc:22 * e + hc + 1],
                                          sem=sAC)
                up_v = {}
                for hc in range(NHC):
                    hsz = HCS[hc]
                    r = hc % 2
                    r4 = hc % 4
                    pp = self.pa if r == 0 else self.pb
                    P.wait("gpsimd", sPE, gu_used4[r4])
                    wv_ = self.dma("gpsimd",
                                   self.wqk.ap()[:, 1024 * r4:1024 * r4 + 8 * hsz].rearrange("p (c h) -> p c h", c=8),
                                   dr["uw"].ap()[e, :, 128 * hc:128 * hc + hsz].rearrange("(c p) h -> p c h", p=128),
                                   sw)
                    P.wait("tensor", sw, wv_)
                    prev = up_v.get(hc - 2)
                    if prev is not None:
                        P.wait("tensor", sDV, prev[0])
                        P.wait("tensor", sAC, prev[1])
                    for cc in range(8):
                        lhs = self.wqk.ap()[:, 1024 * r4 + hsz * cc:1024 * r4 + hsz * (cc + 1)]
                        self.mm(pp[0:hsz, 0:512], lhs, xm4[:, th, cc][:, 0:512],
                                start=(cc == 0), stop=(cc == 7))
                        self.mm(pp[0:hsz, 512:1024], lhs, xm4[:, th, cc][:, 512:1024],
                                start=(cc == 0), stop=(cc == 7),
                                sem=sPE if cc == 7 else None)
                    gu_used4[r4] = self.sPE.v
                    P.wait("scalar", sPE, self.sPE.v)
                    P.wait("scalar", sAC, silu_v[hc])
                    av = self.act(xbf2[:, r][0:hsz, :], pp[0:hsz, :], AF.Identity,
                                  bias=self.gub[0:hsz, 88 + 22 * e + hc:88 + 22 * e + hc + 1],
                                  sem=sAC)
                    P.wait("vector", sAC, av)
                    dv = self.ve("tensor_mul", hid3[:, hc][0:hsz, :], hid3[:, hc][0:hsz, :],
                                 xbf2[:, r][0:hsz, :], sem=sDV)
                    up_v[hc] = (dv, av)
                hid_done = up_v[NHC - 1][0]
                P.wait("tensor", sw, wd_v)
                P.wait("tensor", sDV, hid_done)
                for tc8 in range(8):
                    r = tc8 % 2
                    pp = self.pa if r == 0 else self.pb
                    P.wait("tensor", sDV, down_ev[r])
                    for hc in range(NHC):
                        hsz = HCS[hc]
                        lhs = hid3[:, hc][0:hsz, 128 * tc8:128 * (tc8 + 1)]
                        rhs = (self.wdt.ap()[:, 1024 * hc:1024 * (hc + 1)] if hc < 21
                               else self.wdt.ap()[0:42, 21504:22528])
                        self.mm(pp[:, 0:512], lhs, rhs[0:hsz, 0:512],
                                start=(hc == 0), stop=(hc == 21))
                        self.mm(pp[:, 512:1024], lhs, rhs[0:hsz, 512:1024],
                                start=(hc == 0), stop=(hc == 21),
                                sem=sPE if hc == 21 else None)
                    dv = self.sPE.v
                    P.wait("vector", sPE, dv)
                    if e > 0 or th > 0:
                        P.wait("vector", sDV, resid_v)
                    wcol = self.wts[:, 4 * (8 * th + tc8) + e:4 * (8 * th + tc8) + e + 1]
                    if e == 0:
                        down_ev[r] = self.ve(
                            "tensor_scalar", self.macc[:, 1024 * tc8:1024 * (tc8 + 1)],
                            pp[:], wcol, None, ALU.mult, sem=sDV)
                    else:
                        down_ev[r] = self.ve(
                            "scalar_tensor_tensor", self.macc[:, 1024 * tc8:1024 * (tc8 + 1)],
                            pp[:], wcol, self.macc[:, 1024 * tc8:1024 * (tc8 + 1)],
                            ALU.mult, ALU.add, sem=sDV)
                wd_end = self.sPE.v
            self._ln_store = getattr(self, "_ln_store", [0, 0])
            for tc8 in range(8):
                r = tc8 % 2
                pp = self.pa if r == 0 else self.pb
                P.wait("tensor", sDV, down_ev[r])
                P.wait("tensor", sDV, wtst_v)
                P.wait("tensor", sw, cst_v)
                self.mm(pp[:, 0:512], self.wtst[0:4, 128 * (8 * th + tc8):128 * (8 * th + tc8 + 1)],
                        self.db4[0:4, 0:512], start=True, stop=True)
                bv = self.mm(pp[:, 512:1024], self.wtst[0:4, 128 * (8 * th + tc8):128 * (8 * th + tc8 + 1)],
                             self.db4[0:4, 512:1024], start=True, stop=True, sem=sPE)
                P.wait("vector", sPE, bv)
                down_ev[r] = self.ve("tensor_add", self.macc[:, 1024 * tc8:1024 * (tc8 + 1)],
                                     self.macc[:, 1024 * tc8:1024 * (tc8 + 1)], pp[:], sem=sDV)
            for tc8 in range(8):
                tc = 8 * th + tc8
                resid_v = self.resid_ln_store(src3, dst3, tc,
                                              self.macc[:, 1024 * tc8:1024 * (tc8 + 1)], tc % 2)

    def emit_l1f(self):
        P, dr = self.P, self.dram
        src, dst = self.stream["l1f"]
        sx, st, sw = self.sx, self.st, self.sw
        sPE, sDV, sAC = self.sPE, self.sDV, self.sAC
        src3 = src.ap().rearrange("(k p) d -> p k d", p=128)
        dst3 = dst.ap().rearrange("(k p) d -> p k d", p=128)
        xfm3 = self.xfm.ap().rearrange("p (c t) -> p c t", c=8)

        self.dma("sync", self.srow[0:1, 0:1], dr["iscale"].ap().unsqueeze(0), sx)
        sv = self.dma("sync", self.srow[0:1, 1:2], dr["wscale"].ap().unsqueeze(0), sx)
        P.wait("vector", sx, sv)
        self.ve("reciprocal", self.srow[0:1, 2:3], self.srow[0:1, 0:1])
        rv = self.ve("tensor_mul", self.srow[0:1, 3:4], self.srow[0:1, 0:1],
                     self.srow[0:1, 1:2], sem=sDV)
        P.wait("tensor", sDV, rv)
        P.wait("tensor", self.sPL, 1)
        self.mm(self.pd[:, 0:1], self.ones_r[0:1, 0:128], self.srow[0:1, 2:3])
        bv = self.mm(self.pd[:, 1:2], self.ones_r[0:1, 0:128], self.srow[0:1, 3:4], sem=sPE)
        P.wait("vector", sPE, bv)
        self.ve("tensor_copy", self.sstat[:, 8:9], self.pd[:, 0:1])
        scl_v = self.ve("tensor_copy", self.sstat[:, 9:10], self.pd[:, 1:2], sem=sDV)
        fb = self.dma("sync", self.vrow[:], dr["l1fb"].ap().unsqueeze(0), sx)
        P.wait("tensor", sx, fb)
        P.wait("tensor", sDV, scl_v)
        self.mm(self.pa[:, 0:512], self.ones_r[0:1, 0:128], self.vrow[0:1, 0:512])
        bv = self.mm(self.pa[:, 512:1024], self.ones_r[0:1, 0:128],
                     self.vrow[0:1, 512:1024], sem=sPE)
        P.wait("vector", sPE, bv)
        bfb_v = self.ve("tensor_copy", self.bfb[:], self.pa[:], sem=sDV)
        w8_v = self.dma("gpsimd", self.wfull[:],
                        dr["w8T"].ap().rearrange("(c p) o -> p c o", p=128), sw)

        P.wait("vector", sDV, scl_v)
        self.begin_stage()
        self.stage_fm(src, lambda tc: xfm3[:, :, 128 * tc:128 * (tc + 1)], quant=True)
        stage_done = self.st.v

        self._ln_store = getattr(self, "_ln_store", [0, 0])
        ev_prev = [0, 0]
        for tc in range(NT):
            r = tc % 2
            pp = self.pa if r == 0 else self.pb
            P.wait("tensor", st, stage_done)
            P.wait("tensor", sw, w8_v)
            P.wait("tensor", sDV, max(bfb_v, ev_prev[r]))
            for cc in range(8):
                lhs = xfm3[:, cc][:, 128 * tc:128 * (tc + 1)]
                self.mm(pp[:, 0:512], lhs, self.wfull[:, 1024 * cc:1024 * cc + 512],
                        start=(cc == 0), stop=(cc == 7))
                self.mm(pp[:, 512:1024], lhs, self.wfull[:, 1024 * cc + 512:1024 * (cc + 1)],
                        start=(cc == 0), stop=(cc == 7),
                        sem=sPE if cc == 7 else None)
            mv = self.sPE.v
            P.wait("vector", sPE, mv)
            ev = self.ve("scalar_tensor_tensor", self.scr[:, 0:1024], pp[:],
                         self.sstat[:, 9:10], self.bfb[:], ALU.mult, ALU.add, sem=sDV)
            ev_prev[r] = ev
            self.resid_ln_store(src3, dst3, tc, self.scr[:, 0:1024], r)

    def build(self):
        try:
            self.emit_setup()
            first = True
            for s in self.segs:
                if not first:
                    self.P.barrier()
                first = False
                if s == "l0a":
                    self.emit_attn(0)
                elif s == "moe":
                    self.emit_moe()
                elif s == "l1a":
                    self.emit_attn(1)
                elif s == "l1f":
                    self.emit_l1f()
        except CutEmission:
            pass
        self.P.wait("sync", self.so, self.so.v)
        self.P.barrier()
        self.P.build_block()
        return self.nc



_SEG_INPUTS = {
    "l0a": ["ropeC", "ropeS", "mswap", "e16", "gmask", "gsel", "hsel", "locmask",
            "wq", "wk", "wv", "qb", "kb", "vb"],
    "moe": ["gw", "uw", "dw", "gb", "ub", "db", "rw"],
    "l1a": ["ropeC", "ropeS", "mswap", "e16", "gmask", "gsel", "hsel", "locmask",
            "l1wq", "l1wk", "l1wv", "l1wo", "l1bq", "l1bk", "l1bv", "l1bo"],
    "l1f": ["w8T", "l1fb", "iscale", "wscale"],
}


def host_stage(inputs):
    PQK = _perm_qk()
    g = {}
    g["wq"] = np.ascontiguousarray(np.asarray(inputs["L0_attn_wq_w"])[PQK, :].T)
    g["wk"] = np.ascontiguousarray(np.asarray(inputs["L0_attn_wk_w"])[PQK, :].T)
    g["wv"] = np.ascontiguousarray(np.asarray(inputs["L0_attn_wv_w"]).T)
    g["qb"] = np.ascontiguousarray(np.asarray(inputs["L0_attn_wq_b"])[PQK])
    g["kb"] = np.ascontiguousarray(np.asarray(inputs["L0_attn_wk_b"])[PQK])
    g["vb"] = np.ascontiguousarray(np.asarray(inputs["L0_attn_wv_b"]))
    g["gw"] = np.ascontiguousarray(np.asarray(inputs["L0_ffn_gate_w"]))
    g["uw"] = np.ascontiguousarray(np.asarray(inputs["L0_ffn_up_w"]))
    g["dw"] = np.ascontiguousarray(np.asarray(inputs["L0_ffn_down_w"]))
    g["gb"] = np.ascontiguousarray(np.asarray(inputs["L0_ffn_gate_b"]))
    g["ub"] = np.ascontiguousarray(np.asarray(inputs["L0_ffn_up_b"]))
    g["db"] = np.ascontiguousarray(np.asarray(inputs["L0_ffn_down_b"]))
    g["rw"] = np.ascontiguousarray(np.asarray(inputs["L0_ffn_router_w"]))
    g["l1wq"] = np.ascontiguousarray(np.asarray(inputs["L1_attn_Wq"]))
    g["l1wk"] = np.ascontiguousarray(np.asarray(inputs["L1_attn_Wk"]))
    g["l1wv"] = np.ascontiguousarray(np.asarray(inputs["L1_attn_Wv"]))
    g["l1wo"] = np.ascontiguousarray(np.asarray(inputs["L1_attn_Wo"]))
    g["l1bq"] = np.ascontiguousarray(np.asarray(inputs["L1_attn_bq"]))
    g["l1bk"] = np.ascontiguousarray(np.asarray(inputs["L1_attn_bk"]))
    g["l1bv"] = np.ascontiguousarray(np.asarray(inputs["L1_attn_bv"]))
    g["l1bo"] = np.ascontiguousarray(np.asarray(inputs["L1_attn_bo"]))
    g["w8T"] = np.ascontiguousarray(np.asarray(inputs["L1_ffn_weight_fp8"]).T)
    g["l1fb"] = np.ascontiguousarray(np.asarray(inputs["L1_ffn_bias"]))
    g["iscale"] = np.ascontiguousarray(np.asarray(inputs["L1_ffn_input_scale"]))
    g["wscale"] = np.ascontiguousarray(np.asarray(inputs["L1_ffn_weight_scale"]))
    g["mswap"] = _mswap()
    g["e16"] = np.ascontiguousarray(_e16())
    g["locmask"] = _locmask()
    per = []
    x = np.asarray(inputs["L0_attn_x"]).reshape(B, 2, T, D)
    for c in range(8):
        b, half = c // 2, c % 2
        d = {"x0": np.ascontiguousarray(x[b, half])}
        ct, stt = _rope_tables(half)
        d["ropeC"], d["ropeS"] = ct, stt
        d["gmask"] = _gmask(half)
        d["gsel"] = _gsel(half)
        d["hsel"] = np.full((128, T), half, np.uint8)
        per.append(d)
    return g, per


def _in_maps(k, g, per, segs):
    names = set(["x0"])
    for s in segs:
        names.update(_SEG_INPUTS[s])
    maps = []
    for c in range(8):
        m = {}
        for n in names:
            m[n] = per[c][n] if n in per[c] else g[n]
        maps.append(m)
    return maps


_ALL = ("l0a", "moe", "l1a", "l1f")
MODE = "fused"


def kernel(**inputs):
    g, per = host_stage(inputs)
    if MODE == "fused":
        k = K(_ALL, fused=True)
        nc = k.build()
        maps = _in_maps(k, g, per, _ALL)
        res = run_bass_kernel_spmd(nc, maps, core_ids=list(range(8)))
        outs = [res.results[c]["out"] for c in range(8)]
    else:
        cur = [per[c]["x0"] for c in range(8)]
        for seg in _ALL:
            k = K((seg,), fused=False)
            nc = k.build()
            maps = _in_maps(k, g, per, (seg,))
            for c in range(8):
                maps[c]["x0"] = cur[c]
            res = run_bass_kernel_spmd(nc, maps, core_ids=list(range(8)))
            cur = [res.results[c]["out"] for c in range(8)]
        outs = cur
    out = np.empty((B, SEQ, D), np.float32)
    for c in range(8):
        out[c // 2, (c % 2) * T:(c % 2 + 1) * T, :] = outs[c]
    return out
